# version 12
# baseline (speedup 1.0000x reference)
import math
import numpy as np
from contextlib import ExitStack

import concourse.bass as bass
import concourse.mybir as mybir
from concourse.bass_utils import run_bass_kernel_spmd

F32 = mybir.dt.float32
BF16 = mybir.dt.bfloat16
I8 = mybir.dt.int8
I32 = mybir.dt.int32
U8 = mybir.dt.uint8
AF = mybir.ActivationFunctionType
ALU = mybir.AluOpType
AXX = mybir.AxisListType.X

N, D, INNER, NH, HD = 2048, 2048, 8192, 16, 128
NB, DC, IC = 16, 16, 64
LN_EPS = RMS_EPS = ACT_CLIP = 1e-5
ATT_EPS = 1e-6

_C1 = np.float32(6.28125)
_C2 = np.float32(2 * math.pi - float(_C1))
_C3 = np.float32(2 * math.pi - float(_C1) - float(_C2))
INV2PI = float(np.float32(1.0 / (2 * math.pi)))
_EC = math.log(10000.0) / 64.0
_EC_HI = np.float32(_EC)
_EC_LO = np.float32(_EC - float(_EC_HI))

DT_SIZE = {F32: 4, BF16: 2, I8: 1, I32: 4, U8: 1}


class Arena:
    def __init__(self, ap, cap):
        self.ap, self.cap, self.off, self.hwm = ap, cap, 0, 0

    def alloc(self, shape, dtype, name=""):
        free = 1
        for s in shape[1:]:
            free *= s
        nbytes = free * DT_SIZE[dtype]
        off = (self.off + 31) & ~31
        self.off = off + nbytes
        self.hwm = max(self.hwm, self.off)
        assert self.off <= self.cap, f"SBUF overflow at {name}: {self.off} > {self.cap}"
        return self.at(off, shape, dtype)

    def at(self, off, shape, dtype):
        free = 1
        for s in shape[1:]:
            free *= s
        nbytes = free * DT_SIZE[dtype]
        v = self.ap[0:shape[0], off:off + nbytes].bitcast(dtype)
        if len(shape) > 2:
            dims = " ".join(f"d{i}" for i in range(1, len(shape)))
            kw = {f"d{i}": shape[i] for i in range(1, len(shape))}
            v = v.rearrange(f"p ({dims}) -> p {dims}", **kw)
        return v


class Sem:
    def __init__(self, stack, nc, name):
        self.h = stack.enter_context(nc.semaphore(name))
        self.n = 0


PRELOAD_TBL = False
SERIAL_P1 = [False]
NOROPE = [False]
NOWSTATS = [False]
SERIAL_QKV = [False]


def build_program(phases=99, debug=()):
    nc = bass.Bass("TRN2", target_bir_lowering=False)
    with ExitStack() as st:
        _build(nc, st, phases, debug)
    return nc


def _scope_push(nc, name, _state=[None]):
    if _state[0] is not None:
        _state[0].__exit__(None, None, None)
        _state[0] = None
    if name is None:
        return None
    cm = nc.named_scope(name)
    cm.__enter__()
    _state[0] = cm
    return cm


def _scope_close(_state=[None]):
    pass


def _build(nc, st, phases, debug):
    def dram(name, shape, dtype, kind="Internal"):
        if name in debug and kind == "Internal":
            kind = "ExternalOutput"
        return nc.dram_tensor(name, list(shape), dtype, kind=kind).ap()

    x_d = dram("x", (N, D), F32, "ExternalInput")
    wq_d = dram("wq", (D, D), F32, "ExternalInput")
    wk_d = dram("wk", (D, D), F32, "ExternalInput")
    wv_d = dram("wv", (D, D), F32, "ExternalInput")
    wo_d = dram("wo", (D, D), F32, "ExternalInput")
    w1_d = dram("w1", (INNER, D), F32, "ExternalInput")
    b1_d = dram("b1", (INNER,), F32, "ExternalInput")
    lng_d = dram("lng", (INNER,), F32, "ExternalInput")
    lnb_d = dram("lnb", (INNER,), F32, "ExternalInput")
    w2_d = dram("w2", (D, INNER), F32, "ExternalInput")
    b2_d = dram("b2", (D,), F32, "ExternalInput")
    out_d = dram("out", (N, D), F32, "ExternalOutput")

    xn_d = dram("xn_bf", (N, D), BF16)
    aq_d = dram("aq_bf", (N, D), BF16)
    wqn_d = dram("wqn", (D, D), BF16)
    wkn_d = dram("wkn", (D, D), BF16)
    wvn_d = dram("wvn", (D, D), BF16)
    won_d = dram("won", (D, D), BF16)
    w1q_d = dram("w1q", (INNER, D), BF16)
    w2q_d = dram("w2q", (D, INNER), BF16)
    q_d = dram("q_bf", (N, D), BF16)
    k_d = dram("k_bf", (N, D), BF16)
    v_d = dram("v_bf", (N, D), BF16)
    ot_d = dram("ot_bf", (HD, NH * N), BF16)
    s_d = dram("s_f32", (N, D), F32)
    ha_d = dram("hact", (N, INNER), F32)
    hq_d = dram("hq_bf", (N, INNER), BF16)
    stat_d = dram("stat_b", (4, 128), F32)

    ARENA_B = 193 * 1024
    arena_t = st.enter_context(nc.sbuf_tensor("arena", [128, ARENA_B], U8))
    ps = [st.enter_context(nc.psum_tensor(f"ps{i}", [128, 512], F32)) for i in range(8)]

    A = Arena(arena_t[:], ARENA_B)
    rscale1 = A.alloc((128, NB), F32, "rscale1")
    rscale2 = A.alloc((128, NB), F32, "rscale2")
    alpha1 = A.alloc((128, NB), F32, "alpha1")
    alpha2 = A.alloc((128, NB), F32, "alpha2")
    mw_col = A.alloc((128, 2), F32, "mw_col")
    sw_col = A.alloc((128, 2), F32, "sw_col")
    ones_b = A.alloc((128, 1), BF16, "ones_b")
    onesf_r = A.alloc((1, 128), F32, "onesf_r")
    pio2 = A.alloc((128, 1), F32, "pio2")
    srow = [A.alloc((1, 128), F32, f"srow{i}") for i in range(2)]
    sone = A.alloc((1, 8), F32, "sone")
    ln_sum = A.alloc((128, 256), F32, "ln_sum")
    ln_sq = A.alloc((128, 256), F32, "ln_sq")
    wacc = A.alloc((128, IC), F32, "wacc")
    stt_ = A.alloc((128, 48), F32, "p1stats")
    PBASE = A.off
    TBL_B = 2 * NB * 64 * 4
    tbl_off = (ARENA_B - TBL_B - 31) & ~31
    cos_tm = A.at(tbl_off, (128, NB * 64), F32)
    sin_tm = A.at(tbl_off + NB * 64 * 4, (128, NB * 64), F32)

    def region(use_tbl_space=False):
        A.off = PBASE
        A.cap = ARENA_B if use_tbl_space else tbl_off

    sems = {}
    for sname in ("bar", "ld", "p1a", "p1b", "p1c", "p1d", "p1e", "p1f", "p1g",
                  "wca", "wcb", "dwr", "wst", "wqa", "wqb", "wqc", "wqd", "wqe",
                  "tra", "trb", "mmq", "evq", "rpq", "wrq",
                  "ath", "atm", "ats", "atg", "atv", "ato", "atd", "atb", "atw",
                  "qta", "qtb", "qtc", "qtd",
                  "spm", "spv", "spl", "spw",
                  "f1m", "f1v", "f1s", "f1q", "f1w", "f1d", "f1z",
                  "lqa", "lqb", "lqc", "lqd", "lqe", "lqf", "lqw", "lqz", "lqg", "lqh",
                  "f2m", "f2v", "f2w", "f2h", "f2e", "f2x",
                  "outw", "aux1", "aux2", "aux3", "aux4"):
        sems[sname] = Sem(st, nc, sname)

    block = st.enter_context(nc.Block())
    BAR = [0]

    def barrier():
        BAR[0] += 1
        tgt = 5 * BAR[0]
        for ename in ("sync", "scalar", "vector", "tensor", "gpsimd"):
            def f(e, tgt=tgt):
                e.drain()
                e.sem_inc(sems["bar"].h, 1)
                e.wait_ge(sems["bar"].h, tgt)
            getattr(block, ename)(f)
        sems["bar"].n = tgt

    def SV(i, par=0):
        c = par * 20 + i
        return stt_[:, c:c + 1]


    _scope_cm = _scope_push(nc, "p1")
    region()
    theta = A.alloc((128, 64), F32, "theta")
    posf = A.alloc((128, NB), F32, "posf")
    io64 = A.alloc((128, 64), I32, "io64")
    iopos = A.alloc((128, NB), I32, "iopos")
    angb = A.alloc((128, 64), F32, "angb")
    tmp64 = A.alloc((128, 64), F32, "tmp64")
    k32b = A.alloc((128, 64), I32, "k32b")
    kfb = A.alloc((128, 64), F32, "kfb")
    rsin = A.alloc((128, NB * 64), F32, "rsin")
    rcos = A.alloc((128, NB * 64), F32, "rcos")
    xb = [A.alloc((128, D), F32, f"xb{i}") for i in range(2)]
    xnf = [A.alloc((128, D), F32, f"xnf{i}") for i in range(2)]
    xnb = [A.alloc((128, D), BF16, f"xnb{i}") for i in range(2)]
    aqi = [A.alloc((128, D), I8, f"aqi{i}") for i in range(2)]
    aqb = [A.alloc((128, D), BF16, f"aqb{i}") for i in range(2)]
    dump = A.alloc((128, D), F32, "dump")
    wcast = [A.alloc((128, D), BF16, f"wcast{i}") for i in range(2)]

    x_t = x_d.rearrange("(b p) d -> b p d", p=128)
    xn_t = xn_d.rearrange("(b p) d -> b p d", p=128)
    aq_t = aq_d.rearrange("(b p) d -> b p d", p=128)

    @block.gpsimd
    def _(g):
        g.iota(io64[:], pattern=[[1, 64]], base=0, channel_multiplier=0)
        g.iota(iopos[:], pattern=[[128, NB]], base=0, channel_multiplier=1)
        g.drain()
        g.sem_inc(sems["p1a"].h, 1)
    sems["p1a"].n += 1

    @block.vector
    def _(v):
        v.wait_ge(sems["p1a"].h, 1)
        v.memset(ones_b[:], 1.0)
        v.memset(onesf_r[:], 1.0)
        v.memset(sone[:], 1.0)
        v.memset(pio2[:], float(np.float32(math.pi / 2)))
        v.tensor_copy(out=posf[:], in_=iopos[:])
        v.tensor_copy(out=tmp64[:], in_=io64[:])
        v.drain()
        v.tensor_scalar_mul(out=angb[:], in0=tmp64[:], scalar1=-float(_EC_HI))
        v.drain()
        v.scalar_tensor_tensor(out=angb[:], in0=tmp64[:], scalar=-float(_EC_LO),
                               in1=angb[:], op0=ALU.mult, op1=ALU.add).then_inc(sems["p1b"].h, 1)
    sems["p1b"].n += 1

    @block.scalar
    def _(s):
        s.wait_ge(sems["p1b"].h, 1)
        s.activation(out=theta[:], in_=angb[:], func=AF.Exp).then_inc(sems["p1c"].h, 1)
    sems["p1c"].n += 1

    @block.vector
    def _(v):
        v.wait_ge(sems["p1c"].h, 1)
        for b in range(NB):
            v.tensor_scalar_mul(out=angb[:], in0=theta[:], scalar1=posf[:, b:b + 1])
            v.drain()
            for dst, extra in ((rsin[:, b * 64:(b + 1) * 64], 0.0),
                               (rcos[:, b * 64:(b + 1) * 64], 0.25)):
                if extra:
                    v.tensor_scalar(out=tmp64[:], in0=angb[:], scalar1=INV2PI,
                                    scalar2=extra, op0=ALU.mult, op1=ALU.add)
                else:
                    v.tensor_scalar_mul(out=tmp64[:], in0=angb[:], scalar1=INV2PI)
                v.drain()
                v.tensor_copy(out=k32b[:], in_=tmp64[:])
                v.drain()
                v.tensor_copy(out=kfb[:], in_=k32b[:])
                v.drain()
                v.scalar_tensor_tensor(out=dst, in0=kfb[:], scalar=-float(_C1),
                                       in1=angb[:], op0=ALU.mult, op1=ALU.add)
                v.drain()
                v.scalar_tensor_tensor(out=dst, in0=kfb[:], scalar=-float(_C2),
                                       in1=dst, op0=ALU.mult, op1=ALU.add)
                v.drain()
                v.scalar_tensor_tensor(out=dst, in0=kfb[:], scalar=-float(_C3),
                                       in1=dst, op0=ALU.mult, op1=ALU.add)
                v.drain()
        v.drain()
        v.sem_inc(sems["p1d"].h, 1)
    sems["p1d"].n += 1

    @block.scalar
    def _(s):
        s.wait_ge(sems["p1d"].h, 1)
        s.activation(out=sin_tm[:], in_=rsin[:], func=AF.Sin)
        s.activation(out=cos_tm[:], in_=rcos[:], func=AF.Sin, bias=pio2[:]).then_inc(sems["p1c"].h, 1)
    sems["p1c"].n += 1

    for b in range(NB):
        bb = b % 2
        par = b % 2

        @block.sync
        def _(e, b=b, bb=bb, _dwrn=sems["dwr"].n):
            if SERIAL_P1[0] and b >= 1:
                e.wait_ge(sems["dwr"].h, _dwrn)
            if b >= 2:
                e.wait_ge(sems["wca"].h, b - 1)
            e.dma_start(out=xb[bb][:], in_=x_t[b]).then_inc(sems["ld"].h, 16)
        sems["ld"].n += 16
        ldv = sems["ld"].n

        @block.vector
        def _(v, bb=bb, par=par, ldv=ldv):
            v.wait_ge(sems["ld"].h, ldv)
            v.reduce_sum(out=SV(0, par), in_=xb[bb][:], axis=AXX).then_inc(sems["p1a"].h, 1)
        sems["p1a"].n += 1
        va_ = sems["p1a"].n

        @block.scalar
        def _(s, bb=bb, par=par, ldv=ldv, b=b):
            s.wait_ge(sems["ld"].h, ldv)
            if b >= 1:
                s.wait_ge(sems["p1e"].h, b)
            s.activation(out=dump[:], in_=xb[bb][:], func=AF.Square,
                         accum_out=SV(3, par)).then_inc(sems["p1b"].h, 1)
            s.wait_ge(sems["p1c"].h, 3 + b)
            s.activation(out=SV(7, par), in_=SV(6, par), func=AF.Sqrt).then_inc(sems["p1d"].h, 1)
            s.wait_ge(sems["p1e"].h, b + 1)
            s.activation(out=SV(13, par), in_=SV(12, par), func=AF.Sqrt).then_inc(sems["p1f"].h, 1)
            if b >= 2:
                s.wait_ge(sems["wcb"].h, b - 1)
            s.activation(out=xnf[bb][:], in_=xb[bb][:], func=AF.Identity,
                         scale=SV(8, par), bias=SV(10, par)).then_inc(sems["wqe"].h, 1)
            s.wait_ge(sems["p1g"].h, b + 1)
            s.activation(out=aqi[bb][:], in_=xnf[bb][:], func=AF.Copy,
                         scale=SV(19, par)).then_inc(sems["wca"].h, 1)
        sems["p1b"].n += 1
        sb_ = sems["p1b"].n

        @block.vector
        def _(v, par=par, sb_=sb_, va_=va_):
            v.wait_ge(sems["p1b"].h, sb_)
            v.wait_ge(sems["p1a"].h, va_)
            v.tensor_scalar_mul(out=SV(4, par), in0=SV(0, par), scalar1=1.0 / D)
            v.tensor_scalar_mul(out=SV(5, par), in0=SV(3, par), scalar1=1.0 / D)
            v.drain()
            v.tensor_mul(out=SV(6, par), in0=SV(4, par), in1=SV(4, par))
            v.drain()
            v.tensor_sub(out=SV(5, par), in0=SV(5, par), in1=SV(6, par))
            v.drain()
            v.tensor_scalar_add(out=SV(6, par), in0=SV(5, par), scalar1=LN_EPS).then_inc(sems["p1c"].h, 1)
        sems["p1c"].n += 1
        vc_ = sems["p1c"].n

        sems["p1d"].n += 1
        sd_ = sems["p1d"].n

        @block.vector
        def _(v, b=b, par=par, sd_=sd_):
            v.wait_ge(sems["p1d"].h, sd_)
            v.reciprocal(out=SV(8, par), in_=SV(7, par))
            v.drain()
            v.tensor_mul(out=SV(9, par), in0=SV(8, par), in1=SV(8, par))
            v.drain()
            v.tensor_mul(out=SV(9, par), in0=SV(9, par), in1=SV(6, par))
            v.drain()
            v.tensor_scalar(out=SV(9, par), in0=SV(9, par), scalar1=-0.5, scalar2=1.5,
                            op0=ALU.mult, op1=ALU.add)
            v.drain()
            v.tensor_mul(out=SV(8, par), in0=SV(8, par), in1=SV(9, par))
            v.drain()
            v.tensor_mul(out=SV(10, par), in0=SV(4, par), in1=SV(8, par))
            v.drain()
            v.tensor_scalar_mul(out=SV(10, par), in0=SV(10, par), scalar1=-1.0)
            v.tensor_mul(out=SV(11, par), in0=SV(8, par), in1=SV(8, par))
            v.drain()
            v.tensor_mul(out=SV(11, par), in0=SV(11, par), in1=SV(5, par))
            v.drain()
            v.tensor_scalar_add(out=SV(12, par), in0=SV(11, par), scalar1=RMS_EPS).then_inc(sems["p1e"].h, 1)
        sems["p1e"].n += 1
        ve_ = sems["p1e"].n

        sems["p1f"].n += 1
        sems["wqe"].n += 1
        sf_ = sems["p1f"].n
        sx_ = sems["wqe"].n

        @block.vector
        def _(v, b=b, bb=bb, par=par, sf_=sf_, sx_=sx_):
            v.wait_ge(sems["p1f"].h, sf_)
            v.reciprocal(out=SV(14, par), in_=SV(13, par))
            v.drain()
            v.tensor_mul(out=SV(15, par), in0=SV(14, par), in1=SV(14, par))
            v.drain()
            v.tensor_mul(out=SV(15, par), in0=SV(15, par), in1=SV(12, par))
            v.drain()
            v.tensor_scalar(out=SV(15, par), in0=SV(15, par), scalar1=-0.5, scalar2=1.5,
                            op0=ALU.mult, op1=ALU.add)
            v.drain()
            v.tensor_mul(out=SV(14, par), in0=SV(14, par), in1=SV(15, par))
            v.wait_ge(sems["wqe"].h, sx_)
            v.reduce_max(out=SV(16, par), in_=xnf[bb][:], axis=AXX,
                         apply_absolute_value=True)
            v.drain()
            v.tensor_mul(out=SV(17, par), in0=SV(16, par), in1=SV(14, par))
            v.drain()
            v.tensor_scalar_max(out=SV(17, par), in0=SV(17, par), scalar1=ACT_CLIP)
            v.drain()
            v.reciprocal(out=SV(18, par), in_=SV(17, par))
            v.drain()
            v.tensor_scalar_mul(out=SV(18, par), in0=SV(18, par), scalar1=127.0)
            v.drain()
            v.tensor_mul(out=SV(19, par), in0=SV(14, par), in1=SV(18, par))
            v.tensor_scalar_mul(out=rscale1[:, b:b + 1], in0=SV(17, par),
                                scalar1=1.0 / 127.0).then_inc(sems["p1g"].h, 1)
        sems["p1g"].n += 1
        vg_ = sems["p1g"].n

        sems["wca"].n += 1
        sa_ = sems["wca"].n

        @block.vector
        def _(v, b=b, bb=bb, sa_=sa_):
            v.wait_ge(sems["wca"].h, sa_)
            if b >= 2:
                v.wait_ge(sems["dwr"].h, 32 * (b - 1))
            v.tensor_copy(out=xnb[bb][:], in_=xnf[bb][:])
            v.tensor_copy(out=aqb[bb][:], in_=aqi[bb][:]).then_inc(sems["wcb"].h, 1)
        sems["wcb"].n += 1
        vb_ = sems["wcb"].n

        @block.sync
        def _(e, b=b, bb=bb, vb_=vb_):
            e.wait_ge(sems["wcb"].h, vb_)
            e.dma_start(out=xn_t[b], in_=xnb[bb][:]).then_inc(sems["dwr"].h, 16)
            e.dma_start(out=aq_t[b], in_=aqb[bb][:]).then_inc(sems["dwr"].h, 16)
        sems["dwr"].n += 32

        if b == 0 and "dbg_xb" in debug:
            dbg_xb = dram("dbg_xb", (128, D), F32)
            dbg_st = dram("dbg_st", (128, 48), F32)
            dbg_xnf = dram("dbg_xnf", (128, D), F32)

            @block.sync
            def _(e, vb_=vb_):
                e.wait_ge(sems["wcb"].h, vb_)
                e.dma_start(out=dbg_xb, in_=xb[0][:]).then_inc(sems["dwr"].h, 16)
                e.dma_start(out=dbg_st, in_=stt_[:]).then_inc(sems["dwr"].h, 16)
                e.dma_start(out=dbg_xnf, in_=xnf[0][:]).then_inc(sems["dwr"].h, 16)
            sems["dwr"].n += 48

    wsrc_t = [w.rearrange("(c p) d -> c p d", p=128) for w in (wq_d, wk_d, wv_d, wo_d)]
    wdst_t = [w.rearrange("(c p) d -> c p d", p=128) for w in (wqn_d, wkn_d, wvn_d, won_d)]
    n_wc = 0
    for m in range(4):
        for c in range(DC):
            buf = wcast[n_wc % 2]

            @block.gpsimd
            def _(g, m=m, c=c, buf=buf, n_wc=n_wc):
                if n_wc >= 2:
                    g.wait_ge(sems["wqb"].h, 16 * (n_wc - 1))
                g.dma_start(out=buf[:], in_=wsrc_t[m][c]).then_inc(sems["wqa"].h, 16)
            sems["wqa"].n += 16
            wav = sems["wqa"].n

            @block.scalar
            def _(e, m=m, c=c, buf=buf, wav=wav):
                e.wait_ge(sems["wqa"].h, wav)
                e.dma_start(out=wdst_t[m][c], in_=buf[:]).then_inc(sems["wqb"].h, 16)
            sems["wqb"].n += 16
            n_wc += 1

    @block.sync
    def _(e):
        e.wait_ge(sems["dwr"].h, sems["dwr"].n)

    @block.scalar
    def _(e):
        e.wait_ge(sems["wqb"].h, sems["wqb"].n)

    barrier()
    if phases <= 1:
        _scope_push(nc, None)
        return

    _scope_cm = _scope_push(nc, "p2")
    region()
    xnT = A.alloc((128, DC, N), BF16, "xnT")
    wT = A.alloc((128, DC, D), BF16, "wT")
    qf = [A.alloc((128, D), F32, f"qf{i}") for i in range(2)]
    ro = [A.alloc((128, D), BF16, f"ro{i}") for i in range(2)]
    rt1 = A.alloc((128, NH, 64), F32, "rt1")
    rt2 = A.alloc((128, NH, 64), F32, "rt2")
    wfc = [A.alloc((128, D), F32, f"wfc{i}") for i in range(2)]

    for c in range(DC):
        @block.sync
        def _(e, c=c):
            e.dma_start_transpose(out=xnT[:, c, :], in_=xn_d[:, c * 128:(c + 1) * 128]
                                  ).then_inc(sems["tra"].h, 16)
        sems["tra"].n += 16
    xnT_done = sems["tra"].n

    q_t = q_d.rearrange("(b p) d -> b p d", p=128)
    k_t = k_d.rearrange("(b p) d -> b p d", p=128)
    v_t = v_d.rearrange("(b p) d -> b p d", p=128)

    for ti, (wsrc, dst_t, do_rope) in enumerate(
            ((wqn_d, q_t, True), (wkn_d, k_t, True), (wvn_d, v_t, False))):
        for c in range(DC):
            @block.sync
            def _(e, wsrc=wsrc, c=c, ti=ti):
                if ti > 0 and c == 0:
                    e.wait_ge(sems["mmq"].h, ti * NB)
                e.dma_start_transpose(out=wT[:, c, :], in_=wsrc[:, c * 128:(c + 1) * 128]
                                      ).then_inc(sems["trb"].h, 16)
            sems["trb"].n += 16
        wt_done = sems["trb"].n

        for b in range(NB):
            bb = b % 2
            bank0 = 4 * (b % 2)

            @block.tensor
            def _(t, ti=ti, b=b, bank0=bank0, wt_done=wt_done, xnT_done=xnT_done,
                  _wrqn=sems["wrq"].n):
                if ti == 0 and b == 0:
                    t.wait_ge(sems["tra"].h, xnT_done)
                t.wait_ge(sems["trb"].h, wt_done)
                if SERIAL_QKV[0] and ti * NB + b >= 1:
                    t.wait_ge(sems["wrq"].h, _wrqn)
                if ti * NB + b >= 2:
                    t.wait_ge(sems["evq"].h, ti * NB + b - 1)
                for j in range(4):
                    for dc in range(DC):
                        lhsT = xnT[:, dc, b * 128:(b + 1) * 128]
                        mm = t.matmul(ps[bank0 + j][:], lhsT, wT[:, dc, j * 512:(j + 1) * 512],
                                      start=(dc == 0), stop=(dc == DC - 1))
                mm.then_inc(sems["mmq"].h, 1)
            sems["mmq"].n += 1
            mv = sems["mmq"].n

            if do_rope:
                @block.scalar
                def _(s, b=b, bb=bb, bank0=bank0, mv=mv, ti=ti):
                    s.wait_ge(sems["mmq"].h, mv)
                    if ti * NB + b >= 2:
                        s.wait_ge(sems["rpq"].h, ti * NB + b - 1)
                    for j in range(4):
                        mm = s.activation(out=qf[bb][:, j * 512:(j + 1) * 512],
                                          in_=ps[bank0 + j][:], func=AF.Copy)
                    mm.then_inc(sems["evq"].h, 1)
                sems["evq"].n += 1
                ev = sems["evq"].n

                @block.vector
                def _(v, b=b, bb=bb, ev=ev, ti=ti):
                    v.wait_ge(sems["evq"].h, ev)
                    if ti * NB + b >= 2:
                        v.wait_ge(sems["wrq"].h, 16 * (ti * NB + b - 1))
                    if NOROPE[0]:
                        v.tensor_copy(out=ro[bb][:], in_=qf[bb][:]).then_inc(sems["rpq"].h, 1)
                        return
                    q4 = qf[bb].rearrange("p (h t j) -> p h t j", h=NH, t=2)
                    o4 = ro[bb].rearrange("p (h t j) -> p h t j", h=NH, t=2)
                    cb = cos_tm[:, b * 64:(b + 1) * 64].unsqueeze(1).broadcast_to((128, NH, 64))
                    sb2 = sin_tm[:, b * 64:(b + 1) * 64].unsqueeze(1).broadcast_to((128, NH, 64))
                    ql = q4[:, :, 0, :]
                    qh = q4[:, :, 1, :]
                    v.tensor_tensor(out=o4[:, :, 0, :], in0=ql, in1=cb, op=ALU.mult)
                    v.tensor_tensor(out=o4[:, :, 1, :], in0=ql, in1=sb2, op=ALU.mult)
                    v.tensor_tensor(out=rt1[:], in0=qh, in1=sb2, op=ALU.mult)
                    v.tensor_tensor(out=rt2[:], in0=qh, in1=cb, op=ALU.mult)
                    v.tensor_sub(out=o4[:, :, 0, :], in0=o4[:, :, 0, :], in1=rt1[:])
                    v.tensor_add(out=o4[:, :, 1, :], in0=o4[:, :, 1, :],
                                 in1=rt2[:]).then_inc(sems["rpq"].h, 1)
                sems["rpq"].n += 1
                rv = sems["rpq"].n

                @block.sync
                def _(e, b=b, bb=bb, rv=rv, dst_t=dst_t):
                    e.wait_ge(sems["rpq"].h, rv)
                    e.dma_start(out=dst_t[b], in_=ro[bb][:]).then_inc(sems["wrq"].h, 16)
                sems["wrq"].n += 16
            else:
                @block.scalar
                def _(s, b=b, bb=bb, bank0=bank0, mv=mv, ti=ti):
                    s.wait_ge(sems["mmq"].h, mv)
                    if ti * NB + b >= 2:
                        s.wait_ge(sems["wrq"].h, 16 * (ti * NB + b - 1))
                    for j in range(4):
                        mm = s.activation(out=ro[bb][:, j * 512:(j + 1) * 512],
                                          in_=ps[bank0 + j][:], func=AF.Copy)
                    mm.then_inc(sems["evq"].h, 1)
                sems["evq"].n += 1
                ev = sems["evq"].n

                @block.sync
                def _(e, b=b, bb=bb, ev=ev, dst_t=dst_t):
                    e.wait_ge(sems["evq"].h, ev)
                    e.dma_start(out=dst_t[b], in_=ro[bb][:]).then_inc(sems["wrq"].h, 16)
                sems["wrq"].n += 16

    if "dbg_q0" in debug:
        dbg_q0 = dram("dbg_q0", (128, D), F32)
        dbg_cos = dram("dbg_cos", (128, NB * 64), F32)
        dbg_sin = dram("dbg_sin", (128, NB * 64), F32)
        dbg_xnT = dram("dbg_xnT", (128, N), BF16)
        dbg_wT = dram("dbg_wT", (128, D), BF16)

        @block.sync
        def _(e):
            e.wait_ge(sems["evq"].h, 1)
            e.dma_start(out=dbg_q0, in_=qf[0][:]).then_inc(sems["aux4"].h, 16)
            e.dma_start(out=dbg_cos, in_=cos_tm[:]).then_inc(sems["aux4"].h, 16)
            e.dma_start(out=dbg_sin, in_=sin_tm[:]).then_inc(sems["aux4"].h, 16)
            e.dma_start(out=dbg_xnT, in_=xnT[:, 0, :]).then_inc(sems["aux4"].h, 16)
            e.dma_start(out=dbg_wT, in_=wT[:, 0, :]).then_inc(sems["aux4"].h, 16)
            e.wait_ge(sems["aux4"].h, sems["aux4"].n + 80)
        sems["aux4"].n += 80

    w1_t = w1_d.rearrange("(c p) d -> c p d", p=128)
    w2_t = w2_d.rearrange("(c p) (q i) -> c p q i", p=128, q=4)
    freed = []
    for cc in range(0 if NOWSTATS[0] else 2 * IC):
        is_w1 = cc < IC
        src = w1_t[cc] if is_w1 else w2_t[(cc - IC) // 4, :, (cc - IC) % 4, :]
        buf = wfc[cc % 2]

        @block.gpsimd
        def _(g, cc=cc, src=src, buf=buf):
            if cc >= 2:
                sm, cnt = freed[cc - 2]
                g.wait_ge(sm.h, cnt)
            g.dma_start(out=buf[:], in_=src).then_inc(sems["wqc"].h, 16)
        sems["wqc"].n += 16
        wv_ = sems["wqc"].n

        if is_w1:
            @block.scalar
            def _(s, c=cc, buf=buf, wv_=wv_):
                s.wait_ge(sems["wqc"].h, wv_)
                s.activation(out=buf[:], in_=buf[:], func=AF.Abs,
                             accum_out=wacc[:, c:c + 1]).then_inc(sems["wst"].h, 1)
            sems["wst"].n += 1
            freed.append((sems["wst"], sems["wst"].n))
        else:
            @block.vector
            def _(v, cc=cc, buf=buf, wv_=wv_):
                v.wait_ge(sems["wqc"].h, wv_)
                v.reduce_sum(out=stt_[:, 40:41], in_=buf[:], axis=AXX,
                             apply_absolute_value=True)
                v.drain()
                if cc == IC:
                    v.tensor_copy(out=stt_[:, 41:42], in_=stt_[:, 40:41])
                else:
                    v.tensor_add(out=stt_[:, 41:42], in0=stt_[:, 41:42],
                                 in1=stt_[:, 40:41])
                v.tensor_copy(out=stt_[:, 44:45], in_=stt_[:, 40:41]).then_inc(sems["aux1"].h, 1)
            sems["aux1"].n += 1
            freed.append((sems["aux1"], sems["aux1"].n))

    if NOWSTATS[0]:
        pass
    @block.vector
    def _(v):
        if NOWSTATS[0]:
            v.memset(stt_[:, 41:43], 1.0)
        v.wait_ge(sems["wst"].h, sems["wst"].n)
        v.reduce_sum(out=stt_[:, 42:43], in_=wacc[:], axis=AXX) if not NOWSTATS[0] else None
        v.drain()
        v.drain()
        v.sem_inc(sems["aux2"].h, 1)
    sems["aux2"].n += 1

    @block.sync
    def _(e):
        e.wait_ge(sems["aux2"].h, sems["aux2"].n)
        e.wait_ge(sems["aux1"].h, sems["aux1"].n)
        e.dma_start(out=stat_d[0, :], in_=stt_[:, 42:43]).then_inc(sems["aux3"].h, 16)
        e.dma_start(out=stat_d[1, :], in_=stt_[:, 41:42]).then_inc(sems["aux3"].h, 16)
        e.wait_ge(sems["aux3"].h, 32)
        e.dma_start(out=srow[0][:], in_=stat_d[0:1, :]).then_inc(sems["aux3"].h, 16)
        e.dma_start(out=srow[1][:], in_=stat_d[1:2, :]).then_inc(sems["aux3"].h, 16)
    sems["aux3"].n += 64

    @block.vector
    def _(v):
        v.wait_ge(sems["aux3"].h, sems["aux3"].n)
        v.reduce_sum(out=sone[:, 0:1], in_=srow[0][:], axis=AXX)
        v.reduce_sum(out=sone[:, 1:2], in_=srow[1][:], axis=AXX)
        v.drain()
        v.tensor_scalar_mul(out=sone[:, 0:2], in0=sone[:, 0:2],
                            scalar1=1.0 / float(INNER * D))
        v.drain()
        v.tensor_scalar_max(out=sone[:, 0:2], in0=sone[:, 0:2], scalar1=ACT_CLIP).then_inc(sems["aux4"].h, 1)
    sems["aux4"].n += 1

    @block.sync
    def _(e):
        e.wait_ge(sems["aux4"].h, sems["aux4"].n)
        e.dma_start(out=stat_d[2, 0:2], in_=sone[:, 0:2]).then_inc(sems["aux3"].h, 16)
        e.wait_ge(sems["aux3"].h, sems["aux3"].n + 16)
        e.dma_start(out=mw_col[:],
                    in_=stat_d[2:3, 0:2].broadcast_to((128, 2))).then_inc(sems["aux3"].h, 16)
    sems["aux3"].n += 32

    @block.vector
    def _(v):
        v.wait_ge(sems["aux3"].h, sems["aux3"].n)
        v.reciprocal(out=sw_col[:, 0:1], in_=mw_col[:, 0:1])
        v.reciprocal(out=sw_col[:, 1:2], in_=mw_col[:, 1:2])
        v.drain()
        for b in range(NB):
            mm = v.tensor_mul(out=alpha1[:, b:b + 1], in0=rscale1[:, b:b + 1],
                              in1=mw_col[:, 0:1])
        mm.then_inc(sems["aux4"].h, 1)
    sems["aux4"].n += 1

    @block.sync
    def _(e):
        e.wait_ge(sems["wrq"].h, sems["wrq"].n)
        e.wait_ge(sems["aux4"].h, sems["aux4"].n)

    barrier()
    if phases <= 2:
        _scope_push(nc, None)
        return

    _scope_cm = _scope_push(nc, "p3")
    region()
    qTh = [A.alloc((128, N), BF16, f"qTh{i}") for i in range(2)]
    kTh = [A.alloc((128, N), BF16, f"kTh{i}") for i in range(2)]
    vh = [A.alloc((128, DC, HD), BF16, f"vh{i}") for i in range(2)]
    t2b = [A.alloc((128, 512), F32, f"t2b{i}") for i in range(4)]
    t2m = [A.alloc((128, 512), F32, f"t2m{i}") for i in range(4)]
    st4 = [A.alloc((128, 16, 512), BF16, f"st4_{i}") for i in range(2)]
    pot_sb = [A.alloc((128, 512), F32, f"pot_sb{i}") for i in range(2)]
    lnd = [A.alloc((1, 512), F32, f"lnd{i}") for i in range(2)]
    rde = [A.alloc((1, 512), F32, f"rde{i}") for i in range(2)]
    rden = [A.alloc((1, 512), F32, f"rden{i}") for i in range(2)]
    ot_t = [A.alloc((128, 512), BF16, f"ot_t{i}") for i in range(2)]
    wfq = [A.alloc((128, D), F32, f"wfq{i}") for i in range(2)]
    wi8 = [A.alloc((128, D), I8, f"wi8_{i}") for i in range(2)]
    wbf = [A.alloc((128, D), BF16, f"wbf{i}") for i in range(4)]

    v_r = v_d.rearrange("(kb p) d -> kb p d", p=128)

    if PRELOAD_TBL:
        @block.scalar
        def _(s):
            s.add_instruction(mybir.InstLoadActFuncSet(
                name=s.bass.get_next_instruction_name(),
                act_func_set_id=6, ins=[], outs=[]))

    w1q_t = w1q_d.rearrange("(c p) d -> c p d", p=128)
    w2q_t = w2q_d.rearrange("(c p) (q i) -> c p q i", p=128, q=4)
    CCN = 2 * IC

    def q_srcdst(cc):
        if cc < IC:
            return w1_t[cc], w1q_t[cc], sw_col[:, 0:1]
        c2_, qq = (cc - IC) // 4, (cc - IC) % 4
        return w2_t[c2_, :, qq, :], w2q_t[c2_, :, qq, :], sw_col[:, 1:2]

    def q_load(cc):
        src, _dst, _scl = q_srcdst(cc)

        @block.gpsimd
        def _(g, cc=cc, src=src):
            if cc >= 2:
                g.wait_ge(sems["qta"].h, cc - 1)
            g.dma_start(out=wfq[cc % 2][:], in_=src).then_inc(sems["qtb"].h, 16)
        sems["qtb"].n += 16

    def q_mul(cc):
        _src, _dst, scl = q_srcdst(cc)

        @block.scalar
        def _(s, cc=cc, scl=scl):
            s.wait_ge(sems["qtb"].h, 16 * (cc + 1))
            if cc >= 2:
                s.wait_ge(sems["qtc"].h, cc - 1)
            s.activation(out=wi8[cc % 2][:], in_=wfq[cc % 2][:], func=AF.Copy,
                         scale=scl).then_inc(sems["qta"].h, 1)
        sems["qta"].n += 1

    def q_clip(cc):
        @block.vector
        def _(v, cc=cc):
            v.wait_ge(sems["qta"].h, cc + 1)
            if cc >= 4:
                v.wait_ge(sems["qtd"].h, 16 * (cc - 3))
            v.tensor_scalar(out=wbf[cc % 4][:], in0=wi8[cc % 2][:], scalar1=-1.0,
                            scalar2=1.0, op0=ALU.max,
                            op1=ALU.min).then_inc(sems["qtc"].h, 1)
        sems["qtc"].n += 1

    def q_store(cc):
        _src, dst, _scl = q_srcdst(cc)

        @block.scalar
        def _(e, cc=cc, dst=dst):
            e.wait_ge(sems["qtc"].h, cc + 1)
            e.dma_start(out=dst, in_=wbf[cc % 4][:]).then_inc(sems["qtd"].h, 16)
        sems["qtd"].n += 16

    bmv = {}
    A1B = sems["aux1"].n
    A2B = sems["aux2"].n

    def s_tail(s, gp):
        s.wait_ge(sems["ato"].h, gp + 1)
        if gp >= 2:
            s.wait_ge(sems["atb"].h, gp - 1)
            s.wait_ge(sems["aux2"].h, A2B + gp - 1)
        s.activation(out=lnd[gp % 2][:], in_=ps[6 + gp % 2][0:1, :], func=AF.Ln)
        s.activation(out=pot_sb[gp % 2][:], in_=ps[2 + gp % 2][:],
                     func=AF.Copy).then_inc(sems["aux1"].h, 1)
        s.activation(out=rde[gp % 2][:], in_=lnd[gp % 2][:], func=AF.Exp,
                     scale=-1.0).then_inc(sems["atd"].h, 1)
        sems["aux1"].n += 1
        sems["atd"].n += 1

    def v_tail_min(v, gp):
        v.wait_ge(sems["atd"].h, gp + 1)
        if gp >= 2:
            v.wait_ge(sems["atm"].h, bmv[gp - 2])
        v.tensor_scalar_min(out=rden[gp % 2][:], in0=rde[gp % 2][:],
                            scalar1=1.0 / ATT_EPS).then_inc(sems["aux2"].h, 1)
        sems["aux2"].n += 1

    def v_tail_evac(v, gp):
        v.wait_ge(sems["atm"].h, bmv[gp])
        v.wait_ge(sems["aux1"].h, A1B + gp + 1)
        if gp >= 2:
            v.wait_ge(sems["atw"].h, 16 * (gp - 1))
        v.tensor_tensor(out=ot_t[gp % 2][:], in0=pot_sb[gp % 2][:],
                        in1=ps[6 + gp % 2][:],
                        op=ALU.mult).then_inc(sems["atb"].h, 1)
        sems["atb"].n += 1

    def t_bcast(t, gp):
        t.wait_ge(sems["aux2"].h, A2B + gp + 1)
        t.matmul(ps[6 + gp % 2][:], onesf_r[:], rden[gp % 2][:], start=True,
                 stop=True).then_inc(sems["atm"].h, 1)
        sems["atm"].n += 1
        bmv[gp] = sems["atm"].n

    for h in range(NH):
        hb = h % 2

        @block.sync
        def _(e, h=h, hb=hb):
            if h >= 2:
                e.wait_ge(sems["atd"].h, 4 * (h - 1))
            e.dma_start_transpose(out=qTh[hb][:], in_=q_d[:, h * 128:(h + 1) * 128]
                                  ).then_inc(sems["ath"].h, 16)
            e.dma_start_transpose(out=kTh[hb][:], in_=k_d[:, h * 128:(h + 1) * 128]
                                  ).then_inc(sems["ath"].h, 16)
        sems["ath"].n += 32

        @block.gpsimd
        def _(g, h=h, hb=hb):
            if h >= 2:
                g.wait_ge(sems["atd"].h, 4 * (h - 1))
            for kb in range(DC):
                g.dma_start(out=vh[hb][:, kb, :],
                            in_=v_r[kb, :, h * 128:(h + 1) * 128]
                            ).then_inc(sems["ath"].h, 16)
        sems["ath"].n += 16 * DC
        hload = sems["ath"].n

        tile = 0
        for qb in range(4):
            nkb = 4 * qb + 4
            g_idx = h * 4 + qb
            sset = st4[g_idx % 2]
            ST_BANKS = (0, 1, 4, 5)
            tile0 = tile
            amvs = []
            for kb in range(nkb):
                stb = ps[ST_BANKS[tile % 4]]

                @block.tensor
                def _(t, h=h, hb=hb, qb=qb, kb=kb, stb=stb, hload=hload, tile=tile):
                    if kb == 0 and qb == 0:
                        t.wait_ge(sems["ath"].h, hload)
                    if h * 40 + tile >= 4:
                        t.wait_ge(sems["ats"].h, h * 40 + tile - 3)
                    t.matmul(stb[:], kTh[hb][:, kb * 128:(kb + 1) * 128],
                             qTh[hb][:, qb * 512:(qb + 1) * 512],
                             start=True, stop=True).then_inc(sems["atm"].h, 1)
                sems["atm"].n += 1
                amvs.append(sems["atm"].n)
                tile += 1

            @block.scalar
            def _(s, h=h, qb=qb, nkb=nkb, tile0=tile0, amvs=list(amvs),
                  g_idx=g_idx):
                if g_idx >= 1:
                    s_tail(s, g_idx - 1)
                for kb in range(nkb):
                    t = tile0 + kb
                    s.wait_ge(sems["atm"].h, amvs[kb])
                    if h * 40 + t >= 4:
                        s.wait_ge(sems["atv"].h, h * 40 + t - 3)
                    s.activation(out=t2b[t % 4][:], in_=ps[ST_BANKS[t % 4]][:],
                                 func=AF.Square).then_inc(sems["ats"].h, 1)
            asv0 = sems["ats"].n
            sems["ats"].n += nkb

            agvs = {}
            for kb in range(nkb):
                diag = kb >= 4 * qb
                t = tile0 + kb
                if diag:
                    @block.gpsimd
                    def _(g, qb=qb, kb=kb, t=t, asv=asv0 + kb + 1):
                        g.wait_ge(sems["ats"].h, asv)
                        g.affine_select(t2m[t % 4][:], t2b[t % 4][:],
                                        pattern=[[1, 512]], compare_op=ALU.is_ge,
                                        fill=0.0, base=512 * qb - 128 * kb,
                                        channel_multiplier=-1).then_inc(sems["atg"].h, 1)
                    sems["atg"].n += 1
                agvs[kb] = sems["atg"].n

            avv_all = sems["atv"].n + nkb

            @block.tensor
            def _(t, h=h, hb=hb, qb=qb, nkb=nkb, g_idx=g_idx, sset=sset,
                  avv_all=avv_all):
                pot = ps[2 + g_idx % 2]
                pden = ps[6 + g_idx % 2][0:1, :]
                t.wait_ge(sems["atv"].h, avv_all)
                if g_idx >= 2:
                    t.wait_ge(sems["aux1"].h, A1B + g_idx - 1)
                    t.wait_ge(sems["atb"].h, g_idx - 1)
                for kb in range(nkb):
                    t.matmul(pot[:], vh[hb][:, kb, :], sset[:, kb, :],
                             start=(kb == 0), stop=(kb == nkb - 1))
                for kb in range(nkb):
                    mmd = t.matmul(pden, ones_b[:], sset[:, kb, :],
                                   start=(kb == 0), stop=(kb == nkb - 1))
                mmd.then_inc(sems["ato"].h, 1)
                if g_idx >= 1:
                    t_bcast(t, g_idx - 1)
            sems["ato"].n += 1

            @block.vector
            def _(v, qb=qb, nkb=nkb, tile0=tile0, g_idx=g_idx, sset=sset,
                  asv0=asv0, agvs=dict(agvs), h=h):
                for kb in range(nkb):
                    t = tile0 + kb
                    diag = kb >= 4 * qb
                    if diag:
                        v.wait_ge(sems["atg"].h, agvs[kb])
                        srcbuf = t2m[t % 4][:]
                    else:
                        v.wait_ge(sems["ats"].h, asv0 + kb + 1)
                        srcbuf = t2b[t % 4][:]
                    if kb == 0 and g_idx >= 2:
                        v.wait_ge(sems["ato"].h, g_idx - 1)
                    v.tensor_tensor(out=sset[:, kb, :], in0=srcbuf, in1=srcbuf,
                                    op=ALU.mult).then_inc(sems["atv"].h, 1)
                if g_idx >= 1:
                    v_tail_min(v, g_idx - 1)
                for cc in (2 * g_idx - 2, 2 * g_idx - 1):
                    if 0 <= cc < CCN:
                        v.wait_ge(sems["qta"].h, cc + 1)
                        if cc >= 4:
                            v.wait_ge(sems["qtd"].h, 16 * (cc - 3))
                        v.tensor_scalar(out=wbf[cc % 4][:], in0=wi8[cc % 2][:],
                                        scalar1=-1.0, scalar2=1.0, op0=ALU.max,
                                        op1=ALU.min).then_inc(sems["qtc"].h, 1)
                        sems["qtc"].n += 1
                if g_idx >= 1:
                    v_tail_evac(v, g_idx - 1)
            sems["atv"].n += nkb

            if g_idx >= 1:
                gp_ = g_idx - 1
                hp_, qp_ = gp_ // 4, gp_ % 4

                @block.sync
                def _(e, gp_=gp_, hp_=hp_, qp_=qp_):
                    e.wait_ge(sems["atb"].h, gp_ + 1)
                    e.dma_start(out=ot_d[:, hp_ * N + qp_ * 512:
                                         hp_ * N + (qp_ + 1) * 512],
                                in_=ot_t[gp_ % 2][:]).then_inc(sems["atw"].h, 16)
                sems["atw"].n += 16

            for cc in (2 * g_idx, 2 * g_idx + 1):
                if cc < CCN:
                    q_load(cc)
            for cc in (2 * g_idx - 2, 2 * g_idx - 1):
                if 0 <= cc < CCN:
                    q_mul(cc)
            for cc in (2 * g_idx - 4, 2 * g_idx - 3):
                if 0 <= cc < CCN:
                    q_store(cc)

    G_LAST = NH * 4 - 1

    @block.scalar
    def _(s):
        s_tail(s, G_LAST)

    @block.tensor
    def _(t):
        t_bcast(t, G_LAST)

    @block.vector
    def _(v):
        v_tail_min(v, G_LAST)
        v_tail_evac(v, G_LAST)

    @block.sync
    def _(e):
        e.wait_ge(sems["atb"].h, G_LAST + 1)
        e.dma_start(out=ot_d[:, (G_LAST // 4) * N + (G_LAST % 4) * 512:
                             (G_LAST // 4) * N + (G_LAST % 4 + 1) * 512],
                    in_=ot_t[G_LAST % 2][:]).then_inc(sems["atw"].h, 16)
    sems["atw"].n += 16

    for cc in (2 * G_LAST, 2 * G_LAST + 1):
        q_mul(cc)

    @block.vector
    def _(v):
        for cc in (2 * G_LAST, 2 * G_LAST + 1):
            v.wait_ge(sems["qta"].h, cc + 1)
            v.wait_ge(sems["qtd"].h, 16 * (cc - 3))
            v.tensor_scalar(out=wbf[cc % 4][:], in0=wi8[cc % 2][:],
                            scalar1=-1.0, scalar2=1.0, op0=ALU.max,
                            op1=ALU.min).then_inc(sems["qtc"].h, 1)
            sems["qtc"].n += 1

    for cc in range(2 * G_LAST - 2, CCN):
        q_store(cc)

    @block.sync
    def _(e):
        e.wait_ge(sems["atw"].h, sems["atw"].n)

    @block.scalar
    def _(e):
        e.wait_ge(sems["qtd"].h, sems["qtd"].n)

    barrier()
    if phases <= 3:
        _scope_push(nc, None)
        return

    _scope_cm = _scope_push(nc, "p4")
    region()
    aqT = A.alloc((128, DC, N), BF16, "aqT")
    w1s = A.alloc((128, DC, 2048), BF16, "w1slab")
    b1bc = A.alloc((128, 2048), F32, "b1bc")
    h1f = [A.alloc((128, 512), F32, f"h1f{i}") for i in range(4)]
    slf = [A.alloc((128, 512), F32, f"slf{i}") for i in range(4)]
    haf = [A.alloc((128, 512), F32, f"haf{i}") for i in range(4)]
    f1dump = A.alloc((128, 512), F32, "f1dump")

    for c in range(DC):
        @block.sync
        def _(e, c=c):
            e.dma_start_transpose(out=aqT[:, c, :], in_=aq_d[:, c * 128:(c + 1) * 128]
                                  ).then_inc(sems["tra"].h, 16)
        sems["tra"].n += 16
    aqT_done = sems["tra"].n

    ha_t = ha_d.rearrange("(b p) i -> b p i", p=128)
    b1_r = b1_d.rearrange("(a i) -> a i", a=1)

    tilei = 0
    for Q in range(4):
        for c in range(DC):
            @block.sync
            def _(e, Q=Q, c=c, tilei=tilei):
                if Q > 0:
                    e.wait_ge(sems["f1m"].h, Q * NB)
                e.dma_start_transpose(out=w1s[:, c, :],
                                      in_=w1q_d[Q * 2048:(Q + 1) * 2048, c * 128:(c + 1) * 128]
                                      ).then_inc(sems["trb"].h, 16)
            sems["trb"].n += 16
        w1_done = sems["trb"].n

        @block.sync
        def _(e, Q=Q):
            if Q > 0:
                e.wait_ge(sems["f1v"].h, Q * NB)
            e.dma_start(out=b1bc[:],
                        in_=b1_r[:, Q * 2048:(Q + 1) * 2048].broadcast_to((128, 2048))
                        ).then_inc(sems["f1d"].h, 16)
        sems["f1d"].n += 16
        b1v = sems["f1d"].n

        for b in range(NB):
            bank0 = 4 * (b % 2)

            @block.tensor
            def _(t, Q=Q, b=b, bank0=bank0, w1_done=w1_done, aqT_done=aqT_done):
                if Q == 0 and b == 0:
                    t.wait_ge(sems["tra"].h, aqT_done)
                t.wait_ge(sems["trb"].h, w1_done)
                if Q * NB + b >= 2:
                    t.wait_ge(sems["f1v"].h, Q * NB + b - 1)
                for j in range(4):
                    for dc in range(DC):
                        lhsT = aqT[:, dc, b * 128:(b + 1) * 128]
                        mm = t.matmul(ps[bank0 + j][:], lhsT,
                                      w1s[:, dc, j * 512:(j + 1) * 512],
                                      start=(dc == 0), stop=(dc == DC - 1))
                mm.then_inc(sems["f1m"].h, 1)
            sems["f1m"].n += 1
            fmv = sems["f1m"].n

            blki = Q * NB + b

            @block.vector
            def _(v, Q=Q, b=b, bank0=bank0, fmv=fmv, b1v=b1v, blki=blki):
                v.wait_ge(sems["f1m"].h, fmv)
                v.wait_ge(sems["f1d"].h, b1v)
                if blki >= 1:
                    v.wait_ge(sems["f1z"].h, blki)
                for j in range(4):
                    mm = v.scalar_tensor_tensor(out=h1f[j][:], in0=ps[bank0 + j][:],
                                                scalar=alpha1[:, b:b + 1],
                                                in1=b1bc[:, j * 512:(j + 1) * 512],
                                                op0=ALU.mult, op1=ALU.add)
                mm.then_inc(sems["f1v"].h, 1)
            sems["f1v"].n += 1
            fvv = sems["f1v"].n

            @block.scalar
            def _(s, Q=Q, b=b, fvv=fvv, blki=blki, tile0=tilei):
                s.wait_ge(sems["f1v"].h, fvv)
                if blki >= 1:
                    s.wait_ge(sems["f1q"].h, 4 * blki)
                    s.wait_ge(sems["f1w"].h, 16 * 4 * blki)
                for j in range(4):
                    mm = s.activation(out=slf[j][:], in_=h1f[j][:], func=AF.Silu)
                mm.then_inc(sems["f1z"].h, 1)
                for j in range(4):
                    idx = b * 16 + Q * 4 + j
                    s.activation(out=haf[j][:], in_=slf[j][:], func=AF.Gelu,
                                 accum_out=ln_sum[:, idx:idx + 1]
                                 ).then_inc(sems["f1s"].h, 1)
            sems["f1z"].n += 1
            sems["f1s"].n += 4
            fs_base = sems["f1s"].n - 4

            for j in range(4):
                idx = b * 16 + Q * 4 + j

                @block.vector
                def _(v, j=j, idx=idx, fs_=fs_base + j + 1):
                    v.wait_ge(sems["f1s"].h, fs_)
                    v.scalar_tensor_tensor(out=f1dump[:], in0=haf[j][:], scalar=1.0,
                                           in1=haf[j][:], op0=ALU.mult, op1=ALU.mult,
                                           accum_out=ln_sq[:, idx:idx + 1]
                                           ).then_inc(sems["f1q"].h, 1)
                sems["f1q"].n += 1
                fqv = sems["f1q"].n

                @block.sync
                def _(e, Q=Q, b=b, j=j, fqv=fqv):
                    e.wait_ge(sems["f1q"].h, fqv)
                    e.dma_start(out=ha_t[b][:, Q * 2048 + j * 512: Q * 2048 + (j + 1) * 512],
                                in_=haf[j][:]).then_inc(sems["f1w"].h, 16)
                sems["f1w"].n += 16
                tilei += 1

    @block.sync
    def _(e):
        e.wait_ge(sems["f1w"].h, sems["f1w"].n)

    barrier()
    if phases <= 4:
        _scope_push(nc, None)
        return

    _scope_cm = _scope_push(nc, "p5")
    region()
    gbc = A.alloc((128, INNER), F32, "gbc")
    bbc = A.alloc((128, INNER), F32, "bbc")
    hlf = A.alloc((128, INNER), F32, "hlf")
    ztw0 = A.alloc((128, 2048), F32, "ztw0")
    ztw = [ztw0, ztw0]
    habuf = [A.alloc((128, 2048), F32, f"habuf{i}") for i in range(2)]
    hqi0 = A.alloc((128, 2048), I8, "hqi0")
    hqi = [hqi0, hqi0]
    hqb0 = A.alloc((128, 2048), BF16, "hqb0")
    hqb = [hqb0, hqb0]
    woT = A.alloc((128, NH, 1024), BF16, "woT")
    otsl = [A.alloc((128, NH, 128), BF16, f"otsl{i}") for i in range(2)]
    sev = [A.alloc((128, 1024), F32, f"sev{i}") for i in range(2)]

    lng_r = lng_d.rearrange("(a i) -> a i", a=1)
    lnb_r = lnb_d.rearrange("(a i) -> a i", a=1)
    s_t = s_d.rearrange("(b p) d -> b p d", p=128)
    hq_t = hq_d.rearrange("(b p) i -> b p i", p=128)
    ot_r = ot_d.rearrange("p (h n) -> p h n", h=NH)

    @block.sync
    def _(e):
        e.dma_start(out=gbc[:], in_=lng_r.broadcast_to((128, INNER))
                    ).then_inc(sems["lqa"].h, 16)
        e.dma_start(out=bbc[:], in_=lnb_r.broadcast_to((128, INNER))
                    ).then_inc(sems["lqa"].h, 16)
    sems["lqa"].n += 32
    gb_done = sems["lqa"].n

    for half in range(2):
        for c in range(NH):
            @block.sync
            def _(g, half=half, c=c):
                if half == 1 and c == 0:
                    g.wait_ge(sems["spm"].h, NB)
                g.dma_start_transpose(
                    out=woT[:, c, :],
                    in_=won_d[half * 1024:(half + 1) * 1024, c * 128:(c + 1) * 128]
                ).then_inc(sems["spl"].h, 16)
            sems["spl"].n += 16
        wo_done = sems["spl"].n

        for b in range(NB):
            hb = b % 2
            bank0 = 2 * (b % 2)

            @block.sync
            def _(e, half=half, b=b, hb=hb):
                if half * NB + b >= 2:
                    e.wait_ge(sems["spm"].h, half * NB + b - 1)
                e.dma_start(out=otsl[hb][:], in_=ot_r[:, :, b * 128:(b + 1) * 128]
                            ).then_inc(sems["ath"].h, 16)
            sems["ath"].n += 16
            olv = sems["ath"].n

            @block.tensor
            def _(t, half=half, b=b, hb=hb, bank0=bank0, olv=olv, wo_done=wo_done):
                t.wait_ge(sems["ath"].h, olv)
                t.wait_ge(sems["spl"].h, wo_done)
                if half * NB + b >= 2:
                    t.wait_ge(sems["spv"].h, half * NB + b - 1)
                for j in range(2):
                    for hh in range(NH):
                        lhsT = otsl[hb][:, hh, :]
                        mm = t.matmul(ps[bank0 + j][:], lhsT,
                                      woT[:, hh, j * 512:(j + 1) * 512],
                                      start=(hh == 0), stop=(hh == NH - 1))
                mm.then_inc(sems["spm"].h, 1)
            sems["spm"].n += 1
            smv = sems["spm"].n

            @block.scalar
            def _(s, half=half, b=b, bank0=bank0, smv=smv):
                s.wait_ge(sems["spm"].h, smv)
                if half * NB + b >= 2:
                    s.wait_ge(sems["spw"].h, 16 * (half * NB + b - 1))
                for j in range(2):
                    mm = s.activation(out=sev[b % 2][:, j * 512:(j + 1) * 512],
                                      in_=ps[bank0 + j][:], func=AF.Copy)
                mm.then_inc(sems["spv"].h, 1)
            sems["spv"].n += 1
            spv_ = sems["spv"].n

            @block.sync
            def _(e, half=half, b=b, spv_=spv_):
                e.wait_ge(sems["spv"].h, spv_)
                e.dma_start(out=s_t[b][:, half * 1024:(half + 1) * 1024],
                            in_=sev[b % 2][:]).then_inc(sems["spw"].h, 16)
            sems["spw"].n += 16

    bls1 = A.alloc((128, 128), F32, "bls1")
    blq1 = A.alloc((128, 128), F32, "blq1")
    bls2 = A.alloc((128, 64), F32, "bls2")
    blq2 = A.alloc((128, 64), F32, "blq2")
    bls3 = A.alloc((128, 32), F32, "bls3")
    blq3 = A.alloc((128, 32), F32, "blq3")
    bmean = A.alloc((128, 16), F32, "bmean")
    bmsq = A.alloc((128, 16), F32, "bmsq")
    bve = A.alloc((128, 16), F32, "bve")
    bss = A.alloc((128, 16), F32, "bss")
    br0 = A.alloc((128, 16), F32, "br0")
    bnt = A.alloc((128, 16), F32, "bnt")
    bst_r = A.alloc((128, 16), F32, "bst_r")
    bst_b = A.alloc((128, 16), F32, "bst_b")

    @block.vector
    def _(v):
        for dst, src in ((bls1, ln_sum), (blq1, ln_sq), (None, None),
                         (bls2, bls1), (blq2, blq1), (None, None),
                         (bls3, bls2), (blq3, blq2), (None, None),
                         (bmean, bls3), (bmsq, blq3)):
            if dst is None:
                v.drain()
                continue
            r = src.rearrange("p (k two) -> p k two", two=2)
            v.tensor_tensor(out=dst[:], in0=r[:, :, 0], in1=r[:, :, 1],
                            op=ALU.add)
        v.drain()
        v.tensor_scalar_mul(out=bmean[:], in0=bmean[:], scalar1=1.0 / INNER)
        v.tensor_scalar_mul(out=bmsq[:], in0=bmsq[:], scalar1=1.0 / INNER)
        v.drain()
        v.tensor_mul(out=bve[:], in0=bmean[:], in1=bmean[:])
        v.drain()
        v.tensor_sub(out=bve[:], in0=bmsq[:], in1=bve[:])
        v.drain()
        v.tensor_scalar_add(out=bve[:], in0=bve[:], scalar1=LN_EPS)
        v.drain()
        v.sem_inc(sems["lqb"].h, 16)
    sems["lqb"].n += 16

    @block.scalar
    def _(s, _w=sems["lqb"].n):
        s.wait_ge(sems["lqb"].h, _w)
        s.activation(out=bss[:], in_=bve[:], func=AF.Sqrt)
        s.drain()
        s.sem_inc(sems["lqc"].h, 16)
    sems["lqc"].n += 16

    @block.vector
    def _(v, _w=sems["lqc"].n):
        v.wait_ge(sems["lqc"].h, _w)
        v.reciprocal(out=br0[:], in_=bss[:])
        v.drain()
        v.tensor_mul(out=bnt[:], in0=br0[:], in1=br0[:])
        v.drain()
        v.tensor_mul(out=bnt[:], in0=bnt[:], in1=bve[:])
        v.drain()
        v.tensor_scalar(out=bnt[:], in0=bnt[:], scalar1=-0.5, scalar2=1.5,
                        op0=ALU.mult, op1=ALU.add)
        v.drain()
        v.tensor_mul(out=bst_r[:], in0=br0[:], in1=bnt[:])
        v.drain()
        v.tensor_mul(out=bst_b[:], in0=bmean[:], in1=bst_r[:])
        v.drain()
        v.tensor_scalar_mul(out=bst_b[:], in0=bst_b[:], scalar1=-1.0)
        v.drain()
        v.sem_inc(sems["lqd"].h, 16)
    sems["lqd"].n += 16
    LQ1 = sems["lqd"].n

    for b in range(NB):
        par = b % 2
        ldv2 = LQ1

        hav0 = sems["lqa"].n
        for tcol in range(4):
            tb = tcol % 2
            idx = b * 4 + tcol

            @block.sync
            def _(e, b=b, tcol=tcol, tb=tb, idx=idx):
                if idx >= 2:
                    e.wait_ge(sems["lqz"].h, idx - 1)
                e.dma_start(out=habuf[tb][:],
                            in_=ha_t[b][:, tcol * 2048:(tcol + 1) * 2048]
                            ).then_inc(sems["lqa"].h, 16)
            sems["lqa"].n += 16

        @block.scalar
        def _(s, b=b, par=par, ldv2=ldv2, hav0=hav0):
            for tcol in range(4):
                tb = tcol % 2
                idx = b * 4 + tcol
                s.wait_ge(sems["lqa"].h, hav0 + 16 * (tcol + 1))
                s.wait_ge(sems["lqd"].h, ldv2)
                if idx >= 1:
                    s.wait_ge(sems["lqg"].h, idx)
                s.activation(out=ztw[tb][:], in_=habuf[tb][:], func=AF.Identity,
                             scale=bst_r[:, b:b + 1],
                             bias=bst_b[:, b:b + 1]).then_inc(sems["lqz"].h, 1)
        sems["lqz"].n += 4

        for tcol in range(4):
            tb = tcol % 2
            lev = sems["lqz"].n - 3 + tcol

            @block.vector
            def _(v, b=b, par=par, tcol=tcol, tb=tb, lev=lev, gb_done=gb_done):
                v.wait_ge(sems["lqz"].h, lev)
                v.wait_ge(sems["lqa"].h, gb_done)
                if b >= 1:
                    v.wait_ge(sems["lqf"].h, 8 * b)
                sl = slice(tcol * 2048, (tcol + 1) * 2048)
                v.tensor_tensor(out=ztw[tb][:], in0=ztw[tb][:], in1=gbc[:, sl],
                                op=ALU.mult)
                v.drain()
                v.tensor_tensor(out=hlf[:, sl], in0=ztw[tb][:], in1=bbc[:, sl],
                                op=ALU.add)
                v.drain()
                v.reduce_max(out=SV(9 + tcol, par), in_=hlf[:, sl], axis=AXX,
                             apply_absolute_value=True).then_inc(sems["lqg"].h, 1)
            sems["lqg"].n += 1
        lgv_all = sems["lqg"].n

        @block.scalar
        def _(s, b=b, par=par, lgv_all=lgv_all, lbv2=sems["lqb"].n + 1):
            s.wait_ge(sems["lqg"].h, lgv_all)
            for tcol in range(4):
                tb = tcol % 2
                sl = slice(tcol * 2048, (tcol + 1) * 2048)
                s.activation(out=ztw[tb][:], in_=hlf[:, sl], func=AF.Square,
                             accum_out=SV(13 + tcol, par)).then_inc(sems["lqf"].h, 1)
            s.wait_ge(sems["lqb"].h, lbv2)
            s.activation(out=SV(14, par), in_=SV(13, par), func=AF.Sqrt).then_inc(sems["lqc"].h, 1)
        sems["lqf"].n += 4
        lfv = sems["lqf"].n

        @block.vector
        def _(v, b=b, par=par, lfv=lfv):
            v.wait_ge(sems["lqf"].h, lfv)
            v.tensor_add(out=SV(13, par), in0=SV(13, par), in1=SV(14, par))
            v.tensor_add(out=SV(15, par), in0=SV(15, par), in1=SV(16, par))
            v.drain()
            v.tensor_add(out=SV(13, par), in0=SV(13, par), in1=SV(15, par))
            v.tensor_max(out=SV(9, par), in0=SV(9, par), in1=SV(10, par))
            v.tensor_max(out=SV(11, par), in0=SV(11, par), in1=SV(12, par))
            v.drain()
            v.tensor_max(out=SV(9, par), in0=SV(9, par), in1=SV(11, par))
            v.drain()
            v.tensor_scalar_mul(out=SV(13, par), in0=SV(13, par), scalar1=1.0 / INNER)
            v.drain()
            v.tensor_scalar_add(out=SV(13, par), in0=SV(13, par), scalar1=RMS_EPS).then_inc(sems["lqb"].h, 1)
        sems["lqb"].n += 1
        lbv2 = sems["lqb"].n

        sems["lqc"].n += 1
        lcv2 = sems["lqc"].n

        @block.vector
        def _(v, b=b, par=par, lcv2=lcv2):
            v.wait_ge(sems["lqc"].h, lcv2)
            v.reciprocal(out=SV(15, par), in_=SV(14, par))
            v.drain()
            v.tensor_mul(out=SV(16, par), in0=SV(15, par), in1=SV(15, par))
            v.drain()
            v.tensor_mul(out=SV(16, par), in0=SV(16, par), in1=SV(13, par))
            v.drain()
            v.tensor_scalar(out=SV(16, par), in0=SV(16, par), scalar1=-0.5, scalar2=1.5,
                            op0=ALU.mult, op1=ALU.add)
            v.drain()
            v.tensor_mul(out=SV(15, par), in0=SV(15, par), in1=SV(16, par))
            v.drain()
            v.tensor_mul(out=SV(17, par), in0=SV(9, par), in1=SV(15, par))
            v.drain()
            v.tensor_scalar_max(out=SV(17, par), in0=SV(17, par), scalar1=ACT_CLIP)
            v.drain()
            v.reciprocal(out=SV(18, par), in_=SV(17, par))
            v.drain()
            v.tensor_scalar_mul(out=SV(18, par), in0=SV(18, par), scalar1=127.0)
            v.drain()
            v.tensor_mul(out=SV(19, par), in0=SV(15, par), in1=SV(18, par))
            v.tensor_scalar_mul(out=rscale2[:, b:b + 1], in0=SV(17, par),
                                scalar1=1.0 / 127.0)
            v.drain()
            v.tensor_mul(out=alpha2[:, b:b + 1], in0=rscale2[:, b:b + 1],
                         in1=mw_col[:, 1:2]).then_inc(sems["lqd"].h, 1)
        sems["lqd"].n += 1
        ldv3 = sems["lqd"].n

        @block.scalar
        def _(s, b=b, par=par, ldv3=ldv3):
            s.wait_ge(sems["lqd"].h, ldv3)
            for tcol in range(4):
                tb = tcol % 2
                idx = b * 4 + tcol
                if idx >= 1:
                    s.wait_ge(sems["lqh"].h, idx)
                sl = slice(tcol * 2048, (tcol + 1) * 2048)
                s.activation(out=hqi[tb][:], in_=hlf[:, sl], func=AF.Copy,
                             scale=SV(19, par)).then_inc(sems["lqf"].h, 1)
        sems["lqf"].n += 4

        for tcol in range(4):
            tb = tcol % 2
            idx = b * 4 + tcol
            lfv2 = sems["lqf"].n - 3 + tcol

            @block.vector
            def _(v, tb=tb, lfv2=lfv2, idx=idx):
                v.wait_ge(sems["lqf"].h, lfv2)
                if idx >= 1:
                    v.wait_ge(sems["lqw"].h, 16 * idx)
                v.tensor_copy(out=hqb[tb][:], in_=hqi[tb][:]).then_inc(sems["lqh"].h, 1)
            sems["lqh"].n += 1
            lev3 = sems["lqh"].n

            @block.sync
            def _(e, b=b, tcol=tcol, tb=tb, lev3=lev3):
                e.wait_ge(sems["lqh"].h, lev3)
                e.dma_start(out=hq_t[b][:, tcol * 2048:(tcol + 1) * 2048],
                            in_=hqb[tb][:]).then_inc(sems["lqw"].h, 16)
            sems["lqw"].n += 16

    @block.sync
    def _(e):
        e.wait_ge(sems["lqw"].h, sems["lqw"].n)
        e.wait_ge(sems["spw"].h, sems["spw"].n)

    barrier()
    if phases <= 5:
        _scope_push(nc, None)
        return

    _scope_cm = _scope_push(nc, "p6")
    region()
    acc = A.alloc((128, 4, 2048), F32, "acc")
    w2s = [A.alloc((128, 8, 2048), BF16, f"w2slab{i}") for i in range(2)]
    hqs = [A.alloc((128, 8, 512), BF16, f"hqslab{i}") for i in range(2)]
    b2bc = A.alloc((128, 2048), F32, "b2bc")
    tf_ = A.alloc((128, 2048), F32, "tf")
    gf_ = A.alloc((128, 2048), F32, "gf")
    xrow = [A.alloc((128, 2048), F32, f"xrow{i}") for i in range(2)]
    srow2 = [A.alloc((128, 2048), F32, f"srow2_{i}") for i in range(2)]
    orow0 = A.alloc((128, 2048), F32, "orow0")
    orow = [orow0, orow0]

    b2_r = b2_d.rearrange("(a i) -> a i", a=1)
    aux1_b6 = sems["aux1"].n
    aux2_b6 = sems["aux2"].n

    @block.sync
    def _(e):
        e.dma_start(out=b2bc[:], in_=b2_r.broadcast_to((128, 2048))
                    ).then_inc(sems["f2h"].h, 16)
    sems["f2h"].n += 16
    b2done = sems["f2h"].n

    out_t = out_d.rearrange("(b p) d -> b p d", p=128)

    NG2 = 8
    for nq in range(4):
        for icg in range(NG2):
            gno = nq * NG2 + icg
            wb = w2s[gno % 2]
            hb2 = hqs[gno % 2]
            for i in range(8):
                ic = icg * 8 + i

                @block.sync
                def _(e, nq=nq, i=i, ic=ic, gno=gno, wb=wb, hb2=hb2):
                    if gno >= 2 and i == 0:
                        e.wait_ge(sems["f2m"].h, 4 * (gno - 1))
                    e.dma_start_transpose(out=wb[:, i, :],
                                          in_=w2q_d[:, ic * 128:(ic + 1) * 128]
                                          ).then_inc(sems["f2w"].h, 16)
                    e.dma_start_transpose(out=hb2[:, i, :],
                                          in_=hq_d[nq * 512:(nq + 1) * 512,
                                                   ic * 128:(ic + 1) * 128]
                                          ).then_inc(sems["f2h"].h, 16)
                sems["f2w"].n += 16
                sems["f2h"].n += 16
            ld_w = sems["f2w"].n
            ld_h = sems["f2h"].n

            for bl in range(4):
                bank0 = 4 * (bl % 2)

                @block.tensor
                def _(t, bl=bl, bank0=bank0, ld_w=ld_w, ld_h=ld_h, gno=gno,
                      wb=wb, hb2=hb2):
                    t.wait_ge(sems["f2w"].h, ld_w)
                    t.wait_ge(sems["f2h"].h, ld_h)
                    gidx = gno * 4 + bl
                    if gidx >= 2:
                        t.wait_ge(sems["f2v"].h, 4 * (gidx - 1))
                    for j in range(4):
                        for i in range(8):
                            lhsT = hb2[:, i, bl * 128:(bl + 1) * 128]
                            mm = t.matmul(ps[bank0 + j][:], lhsT,
                                          wb[:, i, j * 512:(j + 1) * 512],
                                          start=(i == 0), stop=(i == 7))
                    mm.then_inc(sems["f2m"].h, 1)
                sems["f2m"].n += 1
                fmv = sems["f2m"].n

                @block.vector
                def _(v, icg=icg, bl=bl, bank0=bank0, fmv=fmv):
                    v.wait_ge(sems["f2m"].h, fmv)
                    for j in range(4):
                        dst = acc[:, bl, j * 512:(j + 1) * 512]
                        if icg == 0:
                            v.tensor_copy(out=dst, in_=ps[bank0 + j][:])
                        else:
                            v.tensor_add(out=dst, in0=dst, in1=ps[bank0 + j][:])
                    v.drain()
                    v.sem_inc(sems["f2v"].h, 4)
                sems["f2v"].n += 4

        for bl in range(4):
            nb_g = nq * 4 + bl
            eb = bl % 2

            @block.sync
            def _(e, nq=nq, bl=bl, eb=eb, nb_g=nb_g):
                if nb_g >= 2:
                    e.wait_ge(sems["f2e"].h, nb_g - 1)
                e.dma_start(out=xrow[eb][:], in_=x_t[nb_g]).then_inc(sems["f2x"].h, 16)
                e.dma_start(out=srow2[eb][:], in_=s_t[nb_g]).then_inc(sems["f2x"].h, 16)
            sems["f2x"].n += 32
            xlv = sems["f2x"].n

            @block.vector
            def _(v, nq=nq, bl=bl, nb_g=nb_g, b2done=b2done, aux2_b6=aux2_b6):
                v.wait_ge(sems["f2h"].h, b2done)
                if nb_g >= 1:
                    v.wait_ge(sems["aux2"].h, aux2_b6 + nb_g)
                v.scalar_tensor_tensor(out=tf_[:], in0=acc[:, bl, :],
                                       scalar=alpha2[:, nb_g:nb_g + 1], in1=b2bc[:],
                                       op0=ALU.mult, op1=ALU.add).then_inc(sems["aux1"].h, 1)
            sems["aux1"].n += 1
            gav = sems["aux1"].n

            @block.scalar
            def _(s, gav=gav, nb_g=nb_g):
                s.wait_ge(sems["aux1"].h, gav)
                if nb_g >= 1:
                    s.wait_ge(sems["f2e"].h, nb_g)
                s.activation(out=gf_[:], in_=tf_[:], func=AF.Silu).then_inc(sems["aux2"].h, 1)
            sems["aux2"].n += 1
            gsv = sems["aux2"].n

            @block.vector
            def _(v, eb=eb, gsv=gsv, xlv=xlv, nb_g=nb_g):
                v.wait_ge(sems["aux2"].h, gsv)
                v.wait_ge(sems["f2x"].h, xlv)
                if nb_g >= 1:
                    v.wait_ge(sems["outw"].h, 16 * nb_g)
                v.tensor_tensor(out=orow[eb][:], in0=gf_[:], in1=srow2[eb][:],
                                op=ALU.mult)
                v.drain()
                v.tensor_add(out=orow[eb][:], in0=orow[eb][:], in1=xrow[eb][:]).then_inc(sems["f2e"].h, 1)
            sems["f2e"].n += 1
            fev = sems["f2e"].n

            @block.sync
            def _(e, eb=eb, nb_g=nb_g, fev=fev):
                e.wait_ge(sems["f2e"].h, fev)
                e.dma_start(out=out_t[nb_g], in_=orow[eb][:]).then_inc(sems["outw"].h, 16)
            sems["outw"].n += 16

    @block.sync
    def _(e):
        e.wait_ge(sems["outw"].h, sems["outw"].n)
    _scope_push(nc, None)


def kernel(L0_attn_x, L0_attn_Wq, L0_attn_Wk, L0_attn_Wv, L0_attn_Wo,
           L0_ffn_w1, L0_ffn_b1, L0_ffn_ln_g, L0_ffn_ln_b, L0_ffn_w2, L0_ffn_b2):
    nc = build_program()
    xs = np.ascontiguousarray(np.asarray(L0_attn_x, dtype=np.float32))
    base = {
        "wq": np.ascontiguousarray(L0_attn_Wq, dtype=np.float32),
        "wk": np.ascontiguousarray(L0_attn_Wk, dtype=np.float32),
        "wv": np.ascontiguousarray(L0_attn_Wv, dtype=np.float32),
        "wo": np.ascontiguousarray(L0_attn_Wo, dtype=np.float32),
        "w1": np.ascontiguousarray(L0_ffn_w1, dtype=np.float32),
        "b1": np.ascontiguousarray(L0_ffn_b1, dtype=np.float32),
        "lng": np.ascontiguousarray(L0_ffn_ln_g, dtype=np.float32),
        "lnb": np.ascontiguousarray(L0_ffn_ln_b, dtype=np.float32),
        "w2": np.ascontiguousarray(L0_ffn_w2, dtype=np.float32),
        "b2": np.ascontiguousarray(L0_ffn_b2, dtype=np.float32),
    }
    in_maps = [dict(base, x=xs[i]) for i in range(8)]
    res = run_bass_kernel_spmd(nc, in_maps, core_ids=list(range(8)))
    return np.stack([np.asarray(r["out"]) for r in res.results], axis=0)


# version 13
# speedup vs baseline: 1.0097x; 1.0097x over previous
import math
import numpy as np
from contextlib import ExitStack

import concourse.bass as bass
import concourse.mybir as mybir
from concourse.bass_utils import run_bass_kernel_spmd

F32 = mybir.dt.float32
BF16 = mybir.dt.bfloat16
I8 = mybir.dt.int8
I32 = mybir.dt.int32
U8 = mybir.dt.uint8
AF = mybir.ActivationFunctionType
ALU = mybir.AluOpType
AXX = mybir.AxisListType.X

N, D, INNER, NH, HD = 2048, 2048, 8192, 16, 128
NB, DC, IC = 16, 16, 64
LN_EPS = RMS_EPS = ACT_CLIP = 1e-5
ATT_EPS = 1e-6

_C1 = np.float32(6.28125)
_C2 = np.float32(2 * math.pi - float(_C1))
_C3 = np.float32(2 * math.pi - float(_C1) - float(_C2))
INV2PI = float(np.float32(1.0 / (2 * math.pi)))
_EC = math.log(10000.0) / 64.0
_EC_HI = np.float32(_EC)
_EC_LO = np.float32(_EC - float(_EC_HI))

DT_SIZE = {F32: 4, BF16: 2, I8: 1, I32: 4, U8: 1}


class Arena:
    def __init__(self, ap, cap):
        self.ap, self.cap, self.off, self.hwm = ap, cap, 0, 0

    def alloc(self, shape, dtype, name=""):
        free = 1
        for s in shape[1:]:
            free *= s
        nbytes = free * DT_SIZE[dtype]
        off = (self.off + 31) & ~31
        self.off = off + nbytes
        self.hwm = max(self.hwm, self.off)
        assert self.off <= self.cap, f"SBUF overflow at {name}: {self.off} > {self.cap}"
        return self.at(off, shape, dtype)

    def at(self, off, shape, dtype):
        free = 1
        for s in shape[1:]:
            free *= s
        nbytes = free * DT_SIZE[dtype]
        v = self.ap[0:shape[0], off:off + nbytes].bitcast(dtype)
        if len(shape) > 2:
            dims = " ".join(f"d{i}" for i in range(1, len(shape)))
            kw = {f"d{i}": shape[i] for i in range(1, len(shape))}
            v = v.rearrange(f"p ({dims}) -> p {dims}", **kw)
        return v


class Sem:
    def __init__(self, stack, nc, name):
        self.h = stack.enter_context(nc.semaphore(name))
        self.n = 0


PRELOAD_TBL = False
SERIAL_P1 = [False]
NOROPE = [False]
NOWSTATS = [False]
SERIAL_QKV = [False]


def build_program(phases=99, debug=()):
    nc = bass.Bass("TRN2", target_bir_lowering=False)
    with ExitStack() as st:
        _build(nc, st, phases, debug)
    return nc


def _scope_push(nc, name, _state=[None]):
    if _state[0] is not None:
        _state[0].__exit__(None, None, None)
        _state[0] = None
    if name is None:
        return None
    cm = nc.named_scope(name)
    cm.__enter__()
    _state[0] = cm
    return cm


def _scope_close(_state=[None]):
    pass


def _build(nc, st, phases, debug):
    def dram(name, shape, dtype, kind="Internal"):
        if name in debug and kind == "Internal":
            kind = "ExternalOutput"
        return nc.dram_tensor(name, list(shape), dtype, kind=kind).ap()

    x_d = dram("x", (N, D), F32, "ExternalInput")
    wq_d = dram("wq", (D, D), F32, "ExternalInput")
    wk_d = dram("wk", (D, D), F32, "ExternalInput")
    wv_d = dram("wv", (D, D), F32, "ExternalInput")
    wo_d = dram("wo", (D, D), F32, "ExternalInput")
    w1_d = dram("w1", (INNER, D), F32, "ExternalInput")
    b1_d = dram("b1", (INNER,), F32, "ExternalInput")
    lng_d = dram("lng", (INNER,), F32, "ExternalInput")
    lnb_d = dram("lnb", (INNER,), F32, "ExternalInput")
    w2_d = dram("w2", (D, INNER), F32, "ExternalInput")
    b2_d = dram("b2", (D,), F32, "ExternalInput")
    out_d = dram("out", (N, D), F32, "ExternalOutput")

    xn_d = dram("xn_bf", (N, D), BF16)
    aq_d = dram("aq_bf", (N, D), BF16)
    wqn_d = dram("wqn", (D, D), BF16)
    wkn_d = dram("wkn", (D, D), BF16)
    wvn_d = dram("wvn", (D, D), BF16)
    won_d = dram("won", (D, D), BF16)
    w1q_d = dram("w1q", (INNER, D), BF16)
    w2q_d = dram("w2q", (D, INNER), BF16)
    q_d = dram("q_bf", (N, D), BF16)
    k_d = dram("k_bf", (N, D), BF16)
    v_d = dram("v_bf", (N, D), BF16)
    ot_d = dram("ot_bf", (HD, NH * N), BF16)
    s_d = dram("s_f32", (N, D), F32)
    ha_d = dram("hact", (N, INNER), F32)
    hq_d = dram("hq_bf", (N, INNER), BF16)
    stat_d = dram("stat_b", (4, 128), F32)

    ARENA_B = 193 * 1024
    arena_t = st.enter_context(nc.sbuf_tensor("arena", [128, ARENA_B], U8))
    ps = [st.enter_context(nc.psum_tensor(f"ps{i}", [128, 512], F32)) for i in range(8)]

    A = Arena(arena_t[:], ARENA_B)
    rscale1 = A.alloc((128, NB), F32, "rscale1")
    rscale2 = A.alloc((128, NB), F32, "rscale2")
    alpha1 = A.alloc((128, NB), F32, "alpha1")
    alpha2 = A.alloc((128, NB), F32, "alpha2")
    mw_col = A.alloc((128, 2), F32, "mw_col")
    sw_col = A.alloc((128, 2), F32, "sw_col")
    ones_b = A.alloc((128, 1), BF16, "ones_b")
    onesf_r = A.alloc((1, 128), F32, "onesf_r")
    pio2 = A.alloc((128, 1), F32, "pio2")
    srow = [A.alloc((1, 128), F32, f"srow{i}") for i in range(2)]
    sone = A.alloc((1, 8), F32, "sone")
    ln_sum = A.alloc((128, 256), F32, "ln_sum")
    ln_sq = A.alloc((128, 256), F32, "ln_sq")
    wacc = A.alloc((128, IC), F32, "wacc")
    stt_ = A.alloc((128, 48), F32, "p1stats")
    PBASE = A.off
    TBL_B = 2 * NB * 64 * 4
    tbl_off = (ARENA_B - TBL_B - 31) & ~31
    cos_tm = A.at(tbl_off, (128, NB * 64), F32)
    sin_tm = A.at(tbl_off + NB * 64 * 4, (128, NB * 64), F32)

    def region(use_tbl_space=False):
        A.off = PBASE
        A.cap = ARENA_B if use_tbl_space else tbl_off

    sems = {}
    for sname in ("bar", "ld", "p1a", "p1b", "p1c", "p1d", "p1e", "p1f", "p1g",
                  "wca", "wcb", "dwr", "wst", "wqa", "wqb", "wqc", "wqd", "wqe",
                  "tra", "trb", "mmq", "evq", "rpq", "wrq",
                  "ath", "atm", "ats", "atg", "atv", "ato", "atd", "atb", "atw",
                  "qta", "qtb", "qtc", "qtd",
                  "spm", "spv", "spl", "spw",
                  "f1m", "f1v", "f1s", "f1q", "f1w", "f1d", "f1z",
                  "lqa", "lqb", "lqc", "lqd", "lqe", "lqf", "lqw", "lqz", "lqg", "lqh",
                  "f2m", "f2v", "f2w", "f2h", "f2e", "f2x",
                  "outw", "aux1", "aux2", "aux3", "aux4"):
        sems[sname] = Sem(st, nc, sname)

    block = st.enter_context(nc.Block())
    BAR = [0]

    def barrier():
        BAR[0] += 1
        tgt = 5 * BAR[0]
        for ename in ("sync", "scalar", "vector", "tensor", "gpsimd"):
            def f(e, tgt=tgt):
                e.drain()
                e.sem_inc(sems["bar"].h, 1)
                e.wait_ge(sems["bar"].h, tgt)
            getattr(block, ename)(f)
        sems["bar"].n = tgt

    def SV(i, par=0):
        c = par * 20 + i
        return stt_[:, c:c + 1]


    _scope_cm = _scope_push(nc, "p1")
    region()
    theta = A.alloc((128, 64), F32, "theta")
    posf = A.alloc((128, NB), F32, "posf")
    io64 = A.alloc((128, 64), I32, "io64")
    iopos = A.alloc((128, NB), I32, "iopos")
    angb = A.alloc((128, 64), F32, "angb")
    tmp64 = A.alloc((128, 64), F32, "tmp64")
    k32b = A.alloc((128, 64), I32, "k32b")
    kfb = A.alloc((128, 64), F32, "kfb")
    rsin = A.alloc((128, NB * 64), F32, "rsin")
    rcos = A.alloc((128, NB * 64), F32, "rcos")
    xb = [A.alloc((128, D), F32, f"xb{i}") for i in range(2)]
    xnf = [A.alloc((128, D), F32, f"xnf{i}") for i in range(2)]
    xnb = [A.alloc((128, D), BF16, f"xnb{i}") for i in range(2)]
    aqi = [A.alloc((128, D), I8, f"aqi{i}") for i in range(2)]
    aqb = [A.alloc((128, D), BF16, f"aqb{i}") for i in range(2)]
    dump = A.alloc((128, D), F32, "dump")
    wcast = [A.alloc((128, D), BF16, f"wcast{i}") for i in range(2)]

    x_t = x_d.rearrange("(b p) d -> b p d", p=128)
    xn_t = xn_d.rearrange("(b p) d -> b p d", p=128)
    aq_t = aq_d.rearrange("(b p) d -> b p d", p=128)

    @block.gpsimd
    def _(g):
        g.iota(io64[:], pattern=[[1, 64]], base=0, channel_multiplier=0)
        g.iota(iopos[:], pattern=[[128, NB]], base=0, channel_multiplier=1)
        g.drain()
        g.sem_inc(sems["p1a"].h, 1)
    sems["p1a"].n += 1

    @block.vector
    def _(v):
        v.wait_ge(sems["p1a"].h, 1)
        v.memset(ones_b[:], 1.0)
        v.memset(onesf_r[:], 1.0)
        v.memset(sone[:], 1.0)
        v.memset(pio2[:], float(np.float32(math.pi / 2)))
        v.tensor_copy(out=posf[:], in_=iopos[:])
        v.tensor_copy(out=tmp64[:], in_=io64[:])
        v.drain()
        v.tensor_scalar_mul(out=angb[:], in0=tmp64[:], scalar1=-float(_EC_HI))
        v.drain()
        v.scalar_tensor_tensor(out=angb[:], in0=tmp64[:], scalar=-float(_EC_LO),
                               in1=angb[:], op0=ALU.mult, op1=ALU.add).then_inc(sems["p1b"].h, 1)
    sems["p1b"].n += 1

    @block.scalar
    def _(s):
        s.wait_ge(sems["p1b"].h, 1)
        s.activation(out=theta[:], in_=angb[:], func=AF.Exp).then_inc(sems["p1c"].h, 1)
    sems["p1c"].n += 1

    @block.vector
    def _(v):
        v.wait_ge(sems["p1c"].h, 1)
        for b in range(NB):
            v.tensor_scalar_mul(out=angb[:], in0=theta[:], scalar1=posf[:, b:b + 1])
            v.drain()
            for dst, extra in ((rsin[:, b * 64:(b + 1) * 64], 0.0),
                               (rcos[:, b * 64:(b + 1) * 64], 0.25)):
                if extra:
                    v.tensor_scalar(out=tmp64[:], in0=angb[:], scalar1=INV2PI,
                                    scalar2=extra, op0=ALU.mult, op1=ALU.add)
                else:
                    v.tensor_scalar_mul(out=tmp64[:], in0=angb[:], scalar1=INV2PI)
                v.drain()
                v.tensor_copy(out=k32b[:], in_=tmp64[:])
                v.drain()
                v.tensor_copy(out=kfb[:], in_=k32b[:])
                v.drain()
                v.scalar_tensor_tensor(out=dst, in0=kfb[:], scalar=-float(_C1),
                                       in1=angb[:], op0=ALU.mult, op1=ALU.add)
                v.drain()
                v.scalar_tensor_tensor(out=dst, in0=kfb[:], scalar=-float(_C2),
                                       in1=dst, op0=ALU.mult, op1=ALU.add)
                v.drain()
                v.scalar_tensor_tensor(out=dst, in0=kfb[:], scalar=-float(_C3),
                                       in1=dst, op0=ALU.mult, op1=ALU.add)
                v.drain()
        v.drain()
        v.sem_inc(sems["p1d"].h, 1)
    sems["p1d"].n += 1

    @block.scalar
    def _(s):
        s.wait_ge(sems["p1d"].h, 1)
        s.activation(out=sin_tm[:], in_=rsin[:], func=AF.Sin)
        s.activation(out=cos_tm[:], in_=rcos[:], func=AF.Sin, bias=pio2[:]).then_inc(sems["p1c"].h, 1)
    sems["p1c"].n += 1

    for b in range(NB):
        bb = b % 2
        par = b % 2

        @block.sync
        def _(e, b=b, bb=bb, _dwrn=sems["dwr"].n):
            if SERIAL_P1[0] and b >= 1:
                e.wait_ge(sems["dwr"].h, _dwrn)
            if b >= 2:
                e.wait_ge(sems["wca"].h, b - 1)
            e.dma_start(out=xb[bb][:], in_=x_t[b]).then_inc(sems["ld"].h, 16)
        sems["ld"].n += 16
        ldv = sems["ld"].n

        @block.vector
        def _(v, bb=bb, par=par, ldv=ldv):
            v.wait_ge(sems["ld"].h, ldv)
            v.reduce_sum(out=SV(0, par), in_=xb[bb][:], axis=AXX).then_inc(sems["p1a"].h, 1)
        sems["p1a"].n += 1
        va_ = sems["p1a"].n

        @block.scalar
        def _(s, bb=bb, par=par, ldv=ldv, b=b):
            s.wait_ge(sems["ld"].h, ldv)
            if b >= 1:
                s.wait_ge(sems["p1e"].h, b)
            s.activation(out=dump[:], in_=xb[bb][:], func=AF.Square,
                         accum_out=SV(3, par)).then_inc(sems["p1b"].h, 1)
            s.wait_ge(sems["p1c"].h, 3 + b)
            s.activation(out=SV(7, par), in_=SV(6, par), func=AF.Sqrt).then_inc(sems["p1d"].h, 1)
            s.wait_ge(sems["p1e"].h, b + 1)
            s.activation(out=SV(13, par), in_=SV(12, par), func=AF.Sqrt).then_inc(sems["p1f"].h, 1)
            if b >= 2:
                s.wait_ge(sems["wcb"].h, b - 1)
            s.activation(out=xnf[bb][:], in_=xb[bb][:], func=AF.Identity,
                         scale=SV(8, par), bias=SV(10, par)).then_inc(sems["wqe"].h, 1)
            s.wait_ge(sems["p1g"].h, b + 1)
            s.activation(out=aqi[bb][:], in_=xnf[bb][:], func=AF.Copy,
                         scale=SV(19, par)).then_inc(sems["wca"].h, 1)
        sems["p1b"].n += 1
        sb_ = sems["p1b"].n

        @block.vector
        def _(v, par=par, sb_=sb_, va_=va_):
            v.wait_ge(sems["p1b"].h, sb_)
            v.wait_ge(sems["p1a"].h, va_)
            v.tensor_scalar_mul(out=SV(4, par), in0=SV(0, par), scalar1=1.0 / D)
            v.tensor_scalar_mul(out=SV(5, par), in0=SV(3, par), scalar1=1.0 / D)
            v.drain()
            v.tensor_mul(out=SV(6, par), in0=SV(4, par), in1=SV(4, par))
            v.drain()
            v.tensor_sub(out=SV(5, par), in0=SV(5, par), in1=SV(6, par))
            v.drain()
            v.tensor_scalar_add(out=SV(6, par), in0=SV(5, par), scalar1=LN_EPS).then_inc(sems["p1c"].h, 1)
        sems["p1c"].n += 1
        vc_ = sems["p1c"].n

        sems["p1d"].n += 1
        sd_ = sems["p1d"].n

        @block.vector
        def _(v, b=b, par=par, sd_=sd_):
            v.wait_ge(sems["p1d"].h, sd_)
            v.reciprocal(out=SV(8, par), in_=SV(7, par))
            v.drain()
            v.tensor_mul(out=SV(9, par), in0=SV(8, par), in1=SV(8, par))
            v.drain()
            v.tensor_mul(out=SV(9, par), in0=SV(9, par), in1=SV(6, par))
            v.drain()
            v.tensor_scalar(out=SV(9, par), in0=SV(9, par), scalar1=-0.5, scalar2=1.5,
                            op0=ALU.mult, op1=ALU.add)
            v.drain()
            v.tensor_mul(out=SV(8, par), in0=SV(8, par), in1=SV(9, par))
            v.drain()
            v.tensor_mul(out=SV(10, par), in0=SV(4, par), in1=SV(8, par))
            v.drain()
            v.tensor_scalar_mul(out=SV(10, par), in0=SV(10, par), scalar1=-1.0)
            v.tensor_mul(out=SV(11, par), in0=SV(8, par), in1=SV(8, par))
            v.drain()
            v.tensor_mul(out=SV(11, par), in0=SV(11, par), in1=SV(5, par))
            v.drain()
            v.tensor_scalar_add(out=SV(12, par), in0=SV(11, par), scalar1=RMS_EPS).then_inc(sems["p1e"].h, 1)
        sems["p1e"].n += 1
        ve_ = sems["p1e"].n

        sems["p1f"].n += 1
        sems["wqe"].n += 1
        sf_ = sems["p1f"].n
        sx_ = sems["wqe"].n

        @block.vector
        def _(v, b=b, bb=bb, par=par, sf_=sf_, sx_=sx_):
            v.wait_ge(sems["p1f"].h, sf_)
            v.reciprocal(out=SV(14, par), in_=SV(13, par))
            v.drain()
            v.tensor_mul(out=SV(15, par), in0=SV(14, par), in1=SV(14, par))
            v.drain()
            v.tensor_mul(out=SV(15, par), in0=SV(15, par), in1=SV(12, par))
            v.drain()
            v.tensor_scalar(out=SV(15, par), in0=SV(15, par), scalar1=-0.5, scalar2=1.5,
                            op0=ALU.mult, op1=ALU.add)
            v.drain()
            v.tensor_mul(out=SV(14, par), in0=SV(14, par), in1=SV(15, par))
            v.wait_ge(sems["wqe"].h, sx_)
            v.reduce_max(out=SV(16, par), in_=xnf[bb][:], axis=AXX,
                         apply_absolute_value=True)
            v.drain()
            v.tensor_mul(out=SV(17, par), in0=SV(16, par), in1=SV(14, par))
            v.drain()
            v.tensor_scalar_max(out=SV(17, par), in0=SV(17, par), scalar1=ACT_CLIP)
            v.drain()
            v.reciprocal(out=SV(18, par), in_=SV(17, par))
            v.drain()
            v.tensor_scalar_mul(out=SV(18, par), in0=SV(18, par), scalar1=127.0)
            v.drain()
            v.tensor_mul(out=SV(19, par), in0=SV(14, par), in1=SV(18, par))
            v.tensor_scalar_mul(out=rscale1[:, b:b + 1], in0=SV(17, par),
                                scalar1=1.0 / 127.0).then_inc(sems["p1g"].h, 1)
        sems["p1g"].n += 1
        vg_ = sems["p1g"].n

        sems["wca"].n += 1
        sa_ = sems["wca"].n

        @block.vector
        def _(v, b=b, bb=bb, sa_=sa_):
            v.wait_ge(sems["wca"].h, sa_)
            if b >= 2:
                v.wait_ge(sems["dwr"].h, 32 * (b - 1))
            v.tensor_copy(out=xnb[bb][:], in_=xnf[bb][:])
            v.tensor_copy(out=aqb[bb][:], in_=aqi[bb][:]).then_inc(sems["wcb"].h, 1)
        sems["wcb"].n += 1
        vb_ = sems["wcb"].n

        @block.sync
        def _(e, b=b, bb=bb, vb_=vb_):
            e.wait_ge(sems["wcb"].h, vb_)
            e.dma_start(out=xn_t[b], in_=xnb[bb][:]).then_inc(sems["dwr"].h, 16)
            e.dma_start(out=aq_t[b], in_=aqb[bb][:]).then_inc(sems["dwr"].h, 16)
        sems["dwr"].n += 32

        if b == 0 and "dbg_xb" in debug:
            dbg_xb = dram("dbg_xb", (128, D), F32)
            dbg_st = dram("dbg_st", (128, 48), F32)
            dbg_xnf = dram("dbg_xnf", (128, D), F32)

            @block.sync
            def _(e, vb_=vb_):
                e.wait_ge(sems["wcb"].h, vb_)
                e.dma_start(out=dbg_xb, in_=xb[0][:]).then_inc(sems["dwr"].h, 16)
                e.dma_start(out=dbg_st, in_=stt_[:]).then_inc(sems["dwr"].h, 16)
                e.dma_start(out=dbg_xnf, in_=xnf[0][:]).then_inc(sems["dwr"].h, 16)
            sems["dwr"].n += 48

    wsrc_t = [w.rearrange("(c p) d -> c p d", p=128) for w in (wq_d, wk_d, wv_d, wo_d)]
    wdst_t = [w.rearrange("(c p) d -> c p d", p=128) for w in (wqn_d, wkn_d, wvn_d, won_d)]
    n_wc = 0
    for m in range(4):
        for c in range(DC):
            buf = wcast[n_wc % 2]

            @block.gpsimd
            def _(g, m=m, c=c, buf=buf, n_wc=n_wc):
                if n_wc >= 2:
                    g.wait_ge(sems["wqb"].h, 16 * (n_wc - 1))
                g.dma_start(out=buf[:], in_=wsrc_t[m][c]).then_inc(sems["wqa"].h, 16)
            sems["wqa"].n += 16
            wav = sems["wqa"].n

            @block.scalar
            def _(e, m=m, c=c, buf=buf, wav=wav):
                e.wait_ge(sems["wqa"].h, wav)
                e.dma_start(out=wdst_t[m][c], in_=buf[:]).then_inc(sems["wqb"].h, 16)
            sems["wqb"].n += 16
            n_wc += 1

    @block.sync
    def _(e):
        e.wait_ge(sems["dwr"].h, sems["dwr"].n)

    @block.scalar
    def _(e):
        e.wait_ge(sems["wqb"].h, sems["wqb"].n)

    barrier()
    if phases <= 1:
        _scope_push(nc, None)
        return

    _scope_cm = _scope_push(nc, "p2")
    region()
    xnT = A.alloc((128, DC, N), BF16, "xnT")
    wT = A.alloc((128, DC, D), BF16, "wT")
    qf = [A.alloc((128, D), F32, f"qf{i}") for i in range(2)]
    ro = [A.alloc((128, D), BF16, f"ro{i}") for i in range(2)]
    rt1 = A.alloc((128, NH, 64), F32, "rt1")
    rt2 = A.alloc((128, NH, 64), F32, "rt2")
    wfc = [A.alloc((128, D), F32, f"wfc{i}") for i in range(2)]

    for c in range(DC):
        @block.sync
        def _(e, c=c):
            e.dma_start_transpose(out=xnT[:, c, :], in_=xn_d[:, c * 128:(c + 1) * 128]
                                  ).then_inc(sems["tra"].h, 16)
        sems["tra"].n += 16
    xnT_done = sems["tra"].n

    q_t = q_d.rearrange("(b p) d -> b p d", p=128)
    k_t = k_d.rearrange("(b p) d -> b p d", p=128)
    v_t = v_d.rearrange("(b p) d -> b p d", p=128)

    for ti, (wsrc, dst_t, do_rope) in enumerate(
            ((wqn_d, q_t, True), (wkn_d, k_t, True), (wvn_d, v_t, False))):
        for c in range(DC):
            @block.sync
            def _(e, wsrc=wsrc, c=c, ti=ti):
                if ti > 0 and c == 0:
                    e.wait_ge(sems["mmq"].h, ti * NB)
                e.dma_start_transpose(out=wT[:, c, :], in_=wsrc[:, c * 128:(c + 1) * 128]
                                      ).then_inc(sems["trb"].h, 16)
            sems["trb"].n += 16
        wt_done = sems["trb"].n

        for b in range(NB):
            bb = b % 2
            bank0 = 4 * (b % 2)

            @block.tensor
            def _(t, ti=ti, b=b, bank0=bank0, wt_done=wt_done, xnT_done=xnT_done,
                  _wrqn=sems["wrq"].n):
                if ti == 0 and b == 0:
                    t.wait_ge(sems["tra"].h, xnT_done)
                t.wait_ge(sems["trb"].h, wt_done)
                if SERIAL_QKV[0] and ti * NB + b >= 1:
                    t.wait_ge(sems["wrq"].h, _wrqn)
                if ti * NB + b >= 2:
                    t.wait_ge(sems["evq"].h, ti * NB + b - 1)
                for j in range(4):
                    for dc in range(DC):
                        lhsT = xnT[:, dc, b * 128:(b + 1) * 128]
                        mm = t.matmul(ps[bank0 + j][:], lhsT, wT[:, dc, j * 512:(j + 1) * 512],
                                      start=(dc == 0), stop=(dc == DC - 1))
                mm.then_inc(sems["mmq"].h, 1)
            sems["mmq"].n += 1
            mv = sems["mmq"].n

            if do_rope:
                @block.scalar
                def _(s, b=b, bb=bb, bank0=bank0, mv=mv, ti=ti):
                    s.wait_ge(sems["mmq"].h, mv)
                    if ti * NB + b >= 2:
                        s.wait_ge(sems["rpq"].h, ti * NB + b - 1)
                    for j in range(4):
                        mm = s.activation(out=qf[bb][:, j * 512:(j + 1) * 512],
                                          in_=ps[bank0 + j][:], func=AF.Copy)
                    mm.then_inc(sems["evq"].h, 1)
                sems["evq"].n += 1
                ev = sems["evq"].n

                @block.vector
                def _(v, b=b, bb=bb, ev=ev, ti=ti):
                    v.wait_ge(sems["evq"].h, ev)
                    if ti * NB + b >= 2:
                        v.wait_ge(sems["wrq"].h, 16 * (ti * NB + b - 1))
                    if NOROPE[0]:
                        v.tensor_copy(out=ro[bb][:], in_=qf[bb][:]).then_inc(sems["rpq"].h, 1)
                        return
                    q4 = qf[bb].rearrange("p (h t j) -> p h t j", h=NH, t=2)
                    o4 = ro[bb].rearrange("p (h t j) -> p h t j", h=NH, t=2)
                    cb = cos_tm[:, b * 64:(b + 1) * 64].unsqueeze(1).broadcast_to((128, NH, 64))
                    sb2 = sin_tm[:, b * 64:(b + 1) * 64].unsqueeze(1).broadcast_to((128, NH, 64))
                    ql = q4[:, :, 0, :]
                    qh = q4[:, :, 1, :]
                    v.tensor_tensor(out=o4[:, :, 0, :], in0=ql, in1=cb, op=ALU.mult)
                    v.tensor_tensor(out=o4[:, :, 1, :], in0=ql, in1=sb2, op=ALU.mult)
                    v.tensor_tensor(out=rt1[:], in0=qh, in1=sb2, op=ALU.mult)
                    v.tensor_tensor(out=rt2[:], in0=qh, in1=cb, op=ALU.mult)
                    v.tensor_sub(out=o4[:, :, 0, :], in0=o4[:, :, 0, :], in1=rt1[:])
                    v.tensor_add(out=o4[:, :, 1, :], in0=o4[:, :, 1, :],
                                 in1=rt2[:]).then_inc(sems["rpq"].h, 1)
                sems["rpq"].n += 1
                rv = sems["rpq"].n

                @block.sync
                def _(e, b=b, bb=bb, rv=rv, dst_t=dst_t):
                    e.wait_ge(sems["rpq"].h, rv)
                    e.dma_start(out=dst_t[b], in_=ro[bb][:]).then_inc(sems["wrq"].h, 16)
                sems["wrq"].n += 16
            else:
                @block.scalar
                def _(s, b=b, bb=bb, bank0=bank0, mv=mv, ti=ti):
                    s.wait_ge(sems["mmq"].h, mv)
                    if ti * NB + b >= 2:
                        s.wait_ge(sems["wrq"].h, 16 * (ti * NB + b - 1))
                    for j in range(4):
                        mm = s.activation(out=ro[bb][:, j * 512:(j + 1) * 512],
                                          in_=ps[bank0 + j][:], func=AF.Copy)
                    mm.then_inc(sems["evq"].h, 1)
                sems["evq"].n += 1
                ev = sems["evq"].n

                @block.sync
                def _(e, b=b, bb=bb, ev=ev, dst_t=dst_t):
                    e.wait_ge(sems["evq"].h, ev)
                    e.dma_start(out=dst_t[b], in_=ro[bb][:]).then_inc(sems["wrq"].h, 16)
                sems["wrq"].n += 16

    if "dbg_q0" in debug:
        dbg_q0 = dram("dbg_q0", (128, D), F32)
        dbg_cos = dram("dbg_cos", (128, NB * 64), F32)
        dbg_sin = dram("dbg_sin", (128, NB * 64), F32)
        dbg_xnT = dram("dbg_xnT", (128, N), BF16)
        dbg_wT = dram("dbg_wT", (128, D), BF16)

        @block.sync
        def _(e):
            e.wait_ge(sems["evq"].h, 1)
            e.dma_start(out=dbg_q0, in_=qf[0][:]).then_inc(sems["aux4"].h, 16)
            e.dma_start(out=dbg_cos, in_=cos_tm[:]).then_inc(sems["aux4"].h, 16)
            e.dma_start(out=dbg_sin, in_=sin_tm[:]).then_inc(sems["aux4"].h, 16)
            e.dma_start(out=dbg_xnT, in_=xnT[:, 0, :]).then_inc(sems["aux4"].h, 16)
            e.dma_start(out=dbg_wT, in_=wT[:, 0, :]).then_inc(sems["aux4"].h, 16)
            e.wait_ge(sems["aux4"].h, sems["aux4"].n + 80)
        sems["aux4"].n += 80

    w1_t = w1_d.rearrange("(c p) d -> c p d", p=128)
    w2_t = w2_d.rearrange("(c p) (q i) -> c p q i", p=128, q=4)
    freed = []
    for cc in range(0 if NOWSTATS[0] else 2 * IC):
        is_w1 = cc < IC
        src = w1_t[cc] if is_w1 else w2_t[(cc - IC) // 4, :, (cc - IC) % 4, :]
        buf = wfc[cc % 2]

        @block.gpsimd
        def _(g, cc=cc, src=src, buf=buf):
            if cc >= 2:
                sm, cnt = freed[cc - 2]
                g.wait_ge(sm.h, cnt)
            g.dma_start(out=buf[:], in_=src).then_inc(sems["wqc"].h, 16)
        sems["wqc"].n += 16
        wv_ = sems["wqc"].n

        if is_w1:
            @block.scalar
            def _(s, c=cc, buf=buf, wv_=wv_):
                s.wait_ge(sems["wqc"].h, wv_)
                s.activation(out=buf[:], in_=buf[:], func=AF.Abs,
                             accum_out=wacc[:, c:c + 1]).then_inc(sems["wst"].h, 1)
            sems["wst"].n += 1
            freed.append((sems["wst"], sems["wst"].n))
        else:
            @block.vector
            def _(v, cc=cc, buf=buf, wv_=wv_):
                v.wait_ge(sems["wqc"].h, wv_)
                v.reduce_sum(out=stt_[:, 40:41], in_=buf[:], axis=AXX,
                             apply_absolute_value=True)
                v.drain()
                if cc == IC:
                    v.tensor_copy(out=stt_[:, 41:42], in_=stt_[:, 40:41])
                else:
                    v.tensor_add(out=stt_[:, 41:42], in0=stt_[:, 41:42],
                                 in1=stt_[:, 40:41])
                v.tensor_copy(out=stt_[:, 44:45], in_=stt_[:, 40:41]).then_inc(sems["aux1"].h, 1)
            sems["aux1"].n += 1
            freed.append((sems["aux1"], sems["aux1"].n))

    if NOWSTATS[0]:
        pass
    @block.vector
    def _(v):
        if NOWSTATS[0]:
            v.memset(stt_[:, 41:43], 1.0)
        v.wait_ge(sems["wst"].h, sems["wst"].n)
        v.reduce_sum(out=stt_[:, 42:43], in_=wacc[:], axis=AXX) if not NOWSTATS[0] else None
        v.drain()
        v.drain()
        v.sem_inc(sems["aux2"].h, 1)
    sems["aux2"].n += 1

    @block.sync
    def _(e):
        e.wait_ge(sems["aux2"].h, sems["aux2"].n)
        e.wait_ge(sems["aux1"].h, sems["aux1"].n)
        e.dma_start(out=stat_d[0, :], in_=stt_[:, 42:43]).then_inc(sems["aux3"].h, 16)
        e.dma_start(out=stat_d[1, :], in_=stt_[:, 41:42]).then_inc(sems["aux3"].h, 16)
        e.wait_ge(sems["aux3"].h, 32)
        e.dma_start(out=srow[0][:], in_=stat_d[0:1, :]).then_inc(sems["aux3"].h, 16)
        e.dma_start(out=srow[1][:], in_=stat_d[1:2, :]).then_inc(sems["aux3"].h, 16)
    sems["aux3"].n += 64

    @block.vector
    def _(v):
        v.wait_ge(sems["aux3"].h, sems["aux3"].n)
        v.reduce_sum(out=sone[:, 0:1], in_=srow[0][:], axis=AXX)
        v.reduce_sum(out=sone[:, 1:2], in_=srow[1][:], axis=AXX)
        v.drain()
        v.tensor_scalar_mul(out=sone[:, 0:2], in0=sone[:, 0:2],
                            scalar1=1.0 / float(INNER * D))
        v.drain()
        v.tensor_scalar_max(out=sone[:, 0:2], in0=sone[:, 0:2], scalar1=ACT_CLIP).then_inc(sems["aux4"].h, 1)
    sems["aux4"].n += 1

    @block.sync
    def _(e):
        e.wait_ge(sems["aux4"].h, sems["aux4"].n)
        e.dma_start(out=stat_d[2, 0:2], in_=sone[:, 0:2]).then_inc(sems["aux3"].h, 16)
        e.wait_ge(sems["aux3"].h, sems["aux3"].n + 16)
        e.dma_start(out=mw_col[:],
                    in_=stat_d[2:3, 0:2].broadcast_to((128, 2))).then_inc(sems["aux3"].h, 16)
    sems["aux3"].n += 32

    @block.vector
    def _(v):
        v.wait_ge(sems["aux3"].h, sems["aux3"].n)
        v.reciprocal(out=sw_col[:, 0:1], in_=mw_col[:, 0:1])
        v.reciprocal(out=sw_col[:, 1:2], in_=mw_col[:, 1:2])
        v.drain()
        for b in range(NB):
            mm = v.tensor_mul(out=alpha1[:, b:b + 1], in0=rscale1[:, b:b + 1],
                              in1=mw_col[:, 0:1])
        mm.then_inc(sems["aux4"].h, 1)
    sems["aux4"].n += 1

    @block.sync
    def _(e):
        e.wait_ge(sems["wrq"].h, sems["wrq"].n)
        e.wait_ge(sems["aux4"].h, sems["aux4"].n)

    barrier()
    if phases <= 2:
        _scope_push(nc, None)
        return

    _scope_cm = _scope_push(nc, "p3")
    region()
    qTh = [A.alloc((128, N), BF16, f"qTh{i}") for i in range(2)]
    kTh = [A.alloc((128, N), BF16, f"kTh{i}") for i in range(2)]
    vh = [A.alloc((128, DC, HD), BF16, f"vh{i}") for i in range(2)]
    t2b = [A.alloc((128, 512), F32, f"t2b{i}") for i in range(4)]
    t2m = [A.alloc((128, 512), F32, f"t2m{i}") for i in range(4)]
    st4 = [A.alloc((128, 16, 512), BF16, f"st4_{i}") for i in range(2)]
    pot_sb = [A.alloc((128, 512), F32, f"pot_sb{i}") for i in range(2)]
    lnd = [A.alloc((1, 512), F32, f"lnd{i}") for i in range(2)]
    rde = [A.alloc((1, 512), F32, f"rde{i}") for i in range(2)]
    rden = [A.alloc((1, 512), F32, f"rden{i}") for i in range(2)]
    ot_t = [A.alloc((128, 512), BF16, f"ot_t{i}") for i in range(2)]
    wfq = [A.alloc((128, D), F32, f"wfq{i}") for i in range(2)]
    wi8 = [A.alloc((128, D), I8, f"wi8_{i}") for i in range(2)]
    wbf = [A.alloc((128, D), BF16, f"wbf{i}") for i in range(4)]

    v_r = v_d.rearrange("(kb p) d -> kb p d", p=128)

    if PRELOAD_TBL:
        @block.scalar
        def _(s):
            s.add_instruction(mybir.InstLoadActFuncSet(
                name=s.bass.get_next_instruction_name(),
                act_func_set_id=6, ins=[], outs=[]))

    w1q_t = w1q_d.rearrange("(c p) d -> c p d", p=128)
    w2q_t = w2q_d.rearrange("(c p) (q i) -> c p q i", p=128, q=4)
    CCN = 2 * IC

    def q_srcdst(cc):
        if cc < IC:
            return w1_t[cc], w1q_t[cc], sw_col[:, 0:1]
        c2_, qq = (cc - IC) // 4, (cc - IC) % 4
        return w2_t[c2_, :, qq, :], w2q_t[c2_, :, qq, :], sw_col[:, 1:2]

    def q_load(cc):
        src, _dst, _scl = q_srcdst(cc)

        @block.gpsimd
        def _(g, cc=cc, src=src):
            if cc >= 2:
                g.wait_ge(sems["qta"].h, cc - 1)
            g.dma_start(out=wfq[cc % 2][:], in_=src).then_inc(sems["qtb"].h, 16)
        sems["qtb"].n += 16

    def q_mul(cc):
        _src, _dst, scl = q_srcdst(cc)

        @block.scalar
        def _(s, cc=cc, scl=scl):
            s.wait_ge(sems["qtb"].h, 16 * (cc + 1))
            if cc >= 2:
                s.wait_ge(sems["qtc"].h, cc - 1)
            s.activation(out=wi8[cc % 2][:], in_=wfq[cc % 2][:], func=AF.Copy,
                         scale=scl).then_inc(sems["qta"].h, 1)
        sems["qta"].n += 1

    def q_clip(cc):
        @block.vector
        def _(v, cc=cc):
            v.wait_ge(sems["qta"].h, cc + 1)
            if cc >= 4:
                v.wait_ge(sems["qtd"].h, 16 * (cc - 3))
            v.tensor_scalar(out=wbf[cc % 4][:], in0=wi8[cc % 2][:], scalar1=-1.0,
                            scalar2=1.0, op0=ALU.max,
                            op1=ALU.min).then_inc(sems["qtc"].h, 1)
        sems["qtc"].n += 1

    def q_store(cc):
        _src, dst, _scl = q_srcdst(cc)

        @block.scalar
        def _(e, cc=cc, dst=dst):
            e.wait_ge(sems["qtc"].h, cc + 1)
            e.dma_start(out=dst, in_=wbf[cc % 4][:]).then_inc(sems["qtd"].h, 16)
        sems["qtd"].n += 16

    bmv = {}
    A1B = sems["aux1"].n
    A2B = sems["aux2"].n

    def s_tail(s, gp):
        s.wait_ge(sems["ato"].h, gp + 1)
        if gp >= 2:
            s.wait_ge(sems["atb"].h, gp - 1)
            s.wait_ge(sems["aux2"].h, A2B + gp - 1)
        s.activation(out=lnd[gp % 2][:], in_=ps[6 + gp % 2][0:1, :], func=AF.Ln)
        s.activation(out=pot_sb[gp % 2][:], in_=ps[2 + gp % 2][:],
                     func=AF.Copy).then_inc(sems["aux1"].h, 1)
        s.activation(out=rde[gp % 2][:], in_=lnd[gp % 2][:], func=AF.Exp,
                     scale=-1.0).then_inc(sems["atd"].h, 1)
        sems["aux1"].n += 1
        sems["atd"].n += 1

    def v_tail_min(v, gp):
        v.wait_ge(sems["atd"].h, gp + 1)
        if gp >= 2:
            v.wait_ge(sems["atm"].h, bmv[gp - 2])
        v.tensor_scalar_min(out=rden[gp % 2][:], in0=rde[gp % 2][:],
                            scalar1=1.0 / ATT_EPS).then_inc(sems["aux2"].h, 1)
        sems["aux2"].n += 1

    def v_tail_evac(v, gp):
        v.wait_ge(sems["atm"].h, bmv[gp])
        v.wait_ge(sems["aux1"].h, A1B + gp + 1)
        if gp >= 2:
            v.wait_ge(sems["atw"].h, 16 * (gp - 1))
        v.tensor_tensor(out=ot_t[gp % 2][:], in0=pot_sb[gp % 2][:],
                        in1=ps[6 + gp % 2][:],
                        op=ALU.mult).then_inc(sems["atb"].h, 1)
        sems["atb"].n += 1

    def t_bcast(t, gp):
        t.wait_ge(sems["aux2"].h, A2B + gp + 1)
        t.matmul(ps[6 + gp % 2][:], onesf_r[:], rden[gp % 2][:], start=True,
                 stop=True).then_inc(sems["atm"].h, 1)
        sems["atm"].n += 1
        bmv[gp] = sems["atm"].n

    for h in range(NH):
        hb = h % 2

        @block.sync
        def _(e, h=h, hb=hb):
            if h >= 2:
                e.wait_ge(sems["atd"].h, 4 * (h - 1))
            e.dma_start_transpose(out=qTh[hb][:], in_=q_d[:, h * 128:(h + 1) * 128]
                                  ).then_inc(sems["ath"].h, 16)
            e.dma_start_transpose(out=kTh[hb][:], in_=k_d[:, h * 128:(h + 1) * 128]
                                  ).then_inc(sems["ath"].h, 16)
        sems["ath"].n += 32

        @block.gpsimd
        def _(g, h=h, hb=hb):
            if h >= 2:
                g.wait_ge(sems["atd"].h, 4 * (h - 1))
            for kb in range(DC):
                g.dma_start(out=vh[hb][:, kb, :],
                            in_=v_r[kb, :, h * 128:(h + 1) * 128]
                            ).then_inc(sems["ath"].h, 16)
        sems["ath"].n += 16 * DC
        hload = sems["ath"].n

        tile = 0
        for qb in range(4):
            nkb = 4 * qb + 4
            g_idx = h * 4 + qb
            sset = st4[g_idx % 2]
            ST_BANKS = (0, 1, 4, 5)
            tile0 = tile
            amvs = []
            for kb in range(nkb):
                stb = ps[ST_BANKS[tile % 4]]

                @block.tensor
                def _(t, h=h, hb=hb, qb=qb, kb=kb, stb=stb, hload=hload, tile=tile):
                    if kb == 0 and qb == 0:
                        t.wait_ge(sems["ath"].h, hload)
                    if h * 40 + tile >= 4:
                        t.wait_ge(sems["ats"].h, h * 40 + tile - 3)
                    t.matmul(stb[:], kTh[hb][:, kb * 128:(kb + 1) * 128],
                             qTh[hb][:, qb * 512:(qb + 1) * 512],
                             start=True, stop=True).then_inc(sems["atm"].h, 1)
                sems["atm"].n += 1
                amvs.append(sems["atm"].n)
                tile += 1

            @block.scalar
            def _(s, h=h, qb=qb, nkb=nkb, tile0=tile0, amvs=list(amvs),
                  g_idx=g_idx):
                if g_idx >= 1:
                    s_tail(s, g_idx - 1)
                for kb in range(nkb):
                    t = tile0 + kb
                    s.wait_ge(sems["atm"].h, amvs[kb])
                    if h * 40 + t >= 4:
                        s.wait_ge(sems["atv"].h, h * 40 + t - 3)
                    s.activation(out=t2b[t % 4][:], in_=ps[ST_BANKS[t % 4]][:],
                                 func=AF.Square).then_inc(sems["ats"].h, 1)
            asv0 = sems["ats"].n
            sems["ats"].n += nkb

            agvs = {}
            for kb in range(nkb):
                diag = kb >= 4 * qb
                t = tile0 + kb
                if diag:
                    @block.gpsimd
                    def _(g, qb=qb, kb=kb, t=t, asv=asv0 + kb + 1):
                        g.wait_ge(sems["ats"].h, asv)
                        g.affine_select(t2m[t % 4][:], t2b[t % 4][:],
                                        pattern=[[1, 512]], compare_op=ALU.is_ge,
                                        fill=0.0, base=512 * qb - 128 * kb,
                                        channel_multiplier=-1).then_inc(sems["atg"].h, 1)
                    sems["atg"].n += 1
                agvs[kb] = sems["atg"].n

            avv_all = sems["atv"].n + nkb

            @block.tensor
            def _(t, h=h, hb=hb, qb=qb, nkb=nkb, g_idx=g_idx, sset=sset,
                  avv_all=avv_all):
                pot = ps[2 + g_idx % 2]
                pden = ps[6 + g_idx % 2][0:1, :]
                t.wait_ge(sems["atv"].h, avv_all)
                if g_idx >= 2:
                    t.wait_ge(sems["aux1"].h, A1B + g_idx - 1)
                    t.wait_ge(sems["atb"].h, g_idx - 1)
                for kb in range(nkb):
                    t.matmul(pot[:], vh[hb][:, kb, :], sset[:, kb, :],
                             start=(kb == 0), stop=(kb == nkb - 1))
                for kb in range(nkb):
                    mmd = t.matmul(pden, ones_b[:], sset[:, kb, :],
                                   start=(kb == 0), stop=(kb == nkb - 1))
                mmd.then_inc(sems["ato"].h, 1)
                if g_idx >= 1:
                    t_bcast(t, g_idx - 1)
            sems["ato"].n += 1

            @block.vector
            def _(v, qb=qb, nkb=nkb, tile0=tile0, g_idx=g_idx, sset=sset,
                  asv0=asv0, agvs=dict(agvs), h=h):
                for kb in range(nkb):
                    t = tile0 + kb
                    diag = kb >= 4 * qb
                    if diag:
                        v.wait_ge(sems["atg"].h, agvs[kb])
                        srcbuf = t2m[t % 4][:]
                    else:
                        v.wait_ge(sems["ats"].h, asv0 + kb + 1)
                        srcbuf = t2b[t % 4][:]
                    if kb == 0 and g_idx >= 2:
                        v.wait_ge(sems["ato"].h, g_idx - 1)
                    v.tensor_tensor(out=sset[:, kb, :], in0=srcbuf, in1=srcbuf,
                                    op=ALU.mult).then_inc(sems["atv"].h, 1)
                if g_idx >= 1:
                    v_tail_min(v, g_idx - 1)
                for cc in (2 * g_idx - 2, 2 * g_idx - 1):
                    if 0 <= cc < CCN:
                        v.wait_ge(sems["qta"].h, cc + 1)
                        if cc >= 4:
                            v.wait_ge(sems["qtd"].h, 16 * (cc - 3))
                        v.tensor_scalar(out=wbf[cc % 4][:], in0=wi8[cc % 2][:],
                                        scalar1=-1.0, scalar2=1.0, op0=ALU.max,
                                        op1=ALU.min).then_inc(sems["qtc"].h, 1)
                        sems["qtc"].n += 1
                if g_idx >= 1:
                    v_tail_evac(v, g_idx - 1)
            sems["atv"].n += nkb

            if g_idx >= 1:
                gp_ = g_idx - 1
                hp_, qp_ = gp_ // 4, gp_ % 4

                @block.sync
                def _(e, gp_=gp_, hp_=hp_, qp_=qp_):
                    e.wait_ge(sems["atb"].h, gp_ + 1)
                    e.dma_start(out=ot_d[:, hp_ * N + qp_ * 512:
                                         hp_ * N + (qp_ + 1) * 512],
                                in_=ot_t[gp_ % 2][:]).then_inc(sems["atw"].h, 16)
                sems["atw"].n += 16

            for cc in (2 * g_idx, 2 * g_idx + 1):
                if cc < CCN:
                    q_load(cc)
            for cc in (2 * g_idx - 2, 2 * g_idx - 1):
                if 0 <= cc < CCN:
                    q_mul(cc)
            for cc in (2 * g_idx - 4, 2 * g_idx - 3):
                if 0 <= cc < CCN:
                    q_store(cc)

    G_LAST = NH * 4 - 1

    @block.scalar
    def _(s):
        s_tail(s, G_LAST)

    @block.tensor
    def _(t):
        t_bcast(t, G_LAST)

    @block.vector
    def _(v):
        v_tail_min(v, G_LAST)
        v_tail_evac(v, G_LAST)

    @block.sync
    def _(e):
        e.wait_ge(sems["atb"].h, G_LAST + 1)
        e.dma_start(out=ot_d[:, (G_LAST // 4) * N + (G_LAST % 4) * 512:
                             (G_LAST // 4) * N + (G_LAST % 4 + 1) * 512],
                    in_=ot_t[G_LAST % 2][:]).then_inc(sems["atw"].h, 16)
    sems["atw"].n += 16

    for cc in (2 * G_LAST, 2 * G_LAST + 1):
        q_mul(cc)

    @block.vector
    def _(v):
        for cc in (2 * G_LAST, 2 * G_LAST + 1):
            v.wait_ge(sems["qta"].h, cc + 1)
            v.wait_ge(sems["qtd"].h, 16 * (cc - 3))
            v.tensor_scalar(out=wbf[cc % 4][:], in0=wi8[cc % 2][:],
                            scalar1=-1.0, scalar2=1.0, op0=ALU.max,
                            op1=ALU.min).then_inc(sems["qtc"].h, 1)
            sems["qtc"].n += 1

    for cc in range(2 * G_LAST - 2, CCN):
        q_store(cc)

    @block.sync
    def _(e):
        e.wait_ge(sems["atw"].h, sems["atw"].n)

    @block.scalar
    def _(e):
        e.wait_ge(sems["qtd"].h, sems["qtd"].n)

    barrier()
    if phases <= 3:
        _scope_push(nc, None)
        return

    _scope_cm = _scope_push(nc, "p4")
    region()
    aqT = A.alloc((128, DC, N), BF16, "aqT")
    w1s = A.alloc((128, DC, 2048), BF16, "w1slab")
    b1bc = A.alloc((128, 2048), F32, "b1bc")
    h1f = [A.alloc((128, 512), F32, f"h1f{i}") for i in range(4)]
    slf = [A.alloc((128, 512), F32, f"slf{i}") for i in range(4)]
    haf = [A.alloc((128, 512), F32, f"haf{i}") for i in range(4)]
    f1dump = A.alloc((128, 512), F32, "f1dump")

    for c in range(DC):
        @block.sync
        def _(e, c=c):
            e.dma_start_transpose(out=aqT[:, c, :], in_=aq_d[:, c * 128:(c + 1) * 128]
                                  ).then_inc(sems["tra"].h, 16)
        sems["tra"].n += 16
    aqT_done = sems["tra"].n

    ha_t = ha_d.rearrange("(b p) i -> b p i", p=128)
    b1_r = b1_d.rearrange("(a i) -> a i", a=1)

    tilei = 0
    for Q in range(4):
        for c in range(DC):
            @block.sync
            def _(e, Q=Q, c=c, tilei=tilei):
                if Q > 0:
                    e.wait_ge(sems["f1m"].h, Q * NB)
                e.dma_start_transpose(out=w1s[:, c, :],
                                      in_=w1q_d[Q * 2048:(Q + 1) * 2048, c * 128:(c + 1) * 128]
                                      ).then_inc(sems["trb"].h, 16)
            sems["trb"].n += 16
        w1_done = sems["trb"].n

        @block.sync
        def _(e, Q=Q):
            if Q > 0:
                e.wait_ge(sems["f1v"].h, Q * NB)
            e.dma_start(out=b1bc[:],
                        in_=b1_r[:, Q * 2048:(Q + 1) * 2048].broadcast_to((128, 2048))
                        ).then_inc(sems["f1d"].h, 16)
        sems["f1d"].n += 16
        b1v = sems["f1d"].n

        for b in range(NB):
            bank0 = 4 * (b % 2)

            @block.tensor
            def _(t, Q=Q, b=b, bank0=bank0, w1_done=w1_done, aqT_done=aqT_done):
                if Q == 0 and b == 0:
                    t.wait_ge(sems["tra"].h, aqT_done)
                t.wait_ge(sems["trb"].h, w1_done)
                if Q * NB + b >= 2:
                    t.wait_ge(sems["f1v"].h, Q * NB + b - 1)
                for j in range(4):
                    for dc in range(DC):
                        lhsT = aqT[:, dc, b * 128:(b + 1) * 128]
                        mm = t.matmul(ps[bank0 + j][:], lhsT,
                                      w1s[:, dc, j * 512:(j + 1) * 512],
                                      start=(dc == 0), stop=(dc == DC - 1))
                mm.then_inc(sems["f1m"].h, 1)
            sems["f1m"].n += 1
            fmv = sems["f1m"].n

            blki = Q * NB + b

            @block.vector
            def _(v, Q=Q, b=b, bank0=bank0, fmv=fmv, b1v=b1v, blki=blki):
                v.wait_ge(sems["f1m"].h, fmv)
                v.wait_ge(sems["f1d"].h, b1v)
                if blki >= 1:
                    v.wait_ge(sems["f1z"].h, blki)
                for j in range(4):
                    mm = v.scalar_tensor_tensor(out=h1f[j][:], in0=ps[bank0 + j][:],
                                                scalar=alpha1[:, b:b + 1],
                                                in1=b1bc[:, j * 512:(j + 1) * 512],
                                                op0=ALU.mult, op1=ALU.add)
                mm.then_inc(sems["f1v"].h, 1)
            sems["f1v"].n += 1
            fvv = sems["f1v"].n

            @block.scalar
            def _(s, Q=Q, b=b, fvv=fvv, blki=blki, tile0=tilei):
                s.wait_ge(sems["f1v"].h, fvv)
                if blki >= 1:
                    s.wait_ge(sems["f1q"].h, 4 * blki)
                    s.wait_ge(sems["f1w"].h, 16 * 4 * blki)
                for j in range(4):
                    mm = s.activation(out=slf[j][:], in_=h1f[j][:], func=AF.Silu)
                mm.then_inc(sems["f1z"].h, 1)
                for j in range(4):
                    idx = b * 16 + Q * 4 + j
                    s.activation(out=haf[j][:], in_=slf[j][:], func=AF.Gelu,
                                 accum_out=ln_sum[:, idx:idx + 1]
                                 ).then_inc(sems["f1s"].h, 1)
            sems["f1z"].n += 1
            sems["f1s"].n += 4
            fs_base = sems["f1s"].n - 4

            for j in range(4):
                idx = b * 16 + Q * 4 + j

                @block.vector
                def _(v, j=j, idx=idx, fs_=fs_base + j + 1):
                    v.wait_ge(sems["f1s"].h, fs_)
                    v.scalar_tensor_tensor(out=f1dump[:], in0=haf[j][:], scalar=1.0,
                                           in1=haf[j][:], op0=ALU.mult, op1=ALU.mult,
                                           accum_out=ln_sq[:, idx:idx + 1]
                                           ).then_inc(sems["f1q"].h, 1)
                sems["f1q"].n += 1
                fqv = sems["f1q"].n

                @block.sync
                def _(e, Q=Q, b=b, j=j, fqv=fqv):
                    e.wait_ge(sems["f1q"].h, fqv)
                    e.dma_start(out=ha_t[b][:, Q * 2048 + j * 512: Q * 2048 + (j + 1) * 512],
                                in_=haf[j][:]).then_inc(sems["f1w"].h, 16)
                sems["f1w"].n += 16
                tilei += 1

    @block.sync
    def _(e):
        e.wait_ge(sems["f1w"].h, sems["f1w"].n)

    barrier()
    if phases <= 4:
        _scope_push(nc, None)
        return

    _scope_cm = _scope_push(nc, "p5")
    region()
    gbc = A.alloc((128, INNER), F32, "gbc")
    bbc = A.alloc((128, INNER), F32, "bbc")
    hlf = A.alloc((128, INNER), F32, "hlf")
    ztw0 = A.alloc((128, 2048), F32, "ztw0")
    ztw = [ztw0, ztw0]
    habuf = [A.alloc((128, 2048), F32, f"habuf{i}") for i in range(2)]
    hqi0 = A.alloc((128, 2048), I8, "hqi0")
    bls1 = A.alloc((128, 128), F32, "bls1")
    blq1 = A.alloc((128, 128), F32, "blq1")
    bls2 = A.alloc((128, 64), F32, "bls2")
    blq2 = A.alloc((128, 64), F32, "blq2")
    bls3 = A.alloc((128, 32), F32, "bls3")
    blq3 = A.alloc((128, 32), F32, "blq3")
    bmean = A.alloc((128, 16), F32, "bmean")
    bmsq = A.alloc((128, 16), F32, "bmsq")
    bve = A.alloc((128, 16), F32, "bve")
    bss = A.alloc((128, 16), F32, "bss")
    br0 = A.alloc((128, 16), F32, "br0")
    bnt = A.alloc((128, 16), F32, "bnt")
    bst_r = A.alloc((128, 16), F32, "bst_r")
    bst_b = A.alloc((128, 16), F32, "bst_b")

    @block.vector
    def _(v):
        for dst, src in ((bls1, ln_sum), (blq1, ln_sq), (None, None),
                         (bls2, bls1), (blq2, blq1), (None, None),
                         (bls3, bls2), (blq3, blq2), (None, None),
                         (bmean, bls3), (bmsq, blq3)):
            if dst is None:
                v.drain()
                continue
            r = src.rearrange("p (k two) -> p k two", two=2)
            v.tensor_tensor(out=dst[:], in0=r[:, :, 0], in1=r[:, :, 1],
                            op=ALU.add)
        v.drain()
        v.tensor_scalar_mul(out=bmean[:], in0=bmean[:], scalar1=1.0 / INNER)
        v.tensor_scalar_mul(out=bmsq[:], in0=bmsq[:], scalar1=1.0 / INNER)
        v.drain()
        v.tensor_mul(out=bve[:], in0=bmean[:], in1=bmean[:])
        v.drain()
        v.tensor_sub(out=bve[:], in0=bmsq[:], in1=bve[:])
        v.drain()
        v.tensor_scalar_add(out=bve[:], in0=bve[:], scalar1=LN_EPS)
        v.drain()
        v.sem_inc(sems["lqb"].h, 16)
    sems["lqb"].n += 16

    @block.scalar
    def _(s, _w=sems["lqb"].n):
        s.wait_ge(sems["lqb"].h, _w)
        s.activation(out=bss[:], in_=bve[:], func=AF.Sqrt)
        s.drain()
        s.sem_inc(sems["lqc"].h, 16)
    sems["lqc"].n += 16

    @block.vector
    def _(v, _w=sems["lqc"].n):
        v.wait_ge(sems["lqc"].h, _w)
        v.reciprocal(out=br0[:], in_=bss[:])
        v.drain()
        v.tensor_mul(out=bnt[:], in0=br0[:], in1=br0[:])
        v.drain()
        v.tensor_mul(out=bnt[:], in0=bnt[:], in1=bve[:])
        v.drain()
        v.tensor_scalar(out=bnt[:], in0=bnt[:], scalar1=-0.5, scalar2=1.5,
                        op0=ALU.mult, op1=ALU.add)
        v.drain()
        v.tensor_mul(out=bst_r[:], in0=br0[:], in1=bnt[:])
        v.drain()
        v.tensor_mul(out=bst_b[:], in0=bmean[:], in1=bst_r[:])
        v.drain()
        v.tensor_scalar_mul(out=bst_b[:], in0=bst_b[:], scalar1=-1.0)
        v.drain()
        v.sem_inc(sems["lqd"].h, 16)
    sems["lqd"].n += 16
    LQ1 = sems["lqd"].n
    hqi = [hqi0, hqi0]
    hqb0 = A.alloc((128, 2048), BF16, "hqb0")
    hqb = [hqb0, hqb0]
    woT = A.alloc((128, NH, 1024), BF16, "woT")
    otsl = [A.alloc((128, NH, 128), BF16, f"otsl{i}") for i in range(2)]
    sev = [A.alloc((128, 1024), F32, f"sev{i}") for i in range(2)]

    lng_r = lng_d.rearrange("(a i) -> a i", a=1)
    lnb_r = lnb_d.rearrange("(a i) -> a i", a=1)
    s_t = s_d.rearrange("(b p) d -> b p d", p=128)
    hq_t = hq_d.rearrange("(b p) i -> b p i", p=128)
    ot_r = ot_d.rearrange("p (h n) -> p h n", h=NH)

    @block.sync
    def _(e):
        e.dma_start(out=gbc[:], in_=lng_r.broadcast_to((128, INNER))
                    ).then_inc(sems["lqa"].h, 16)
        e.dma_start(out=bbc[:], in_=lnb_r.broadcast_to((128, INNER))
                    ).then_inc(sems["lqa"].h, 16)
    sems["lqa"].n += 32
    gb_done = sems["lqa"].n

    for half in range(2):
        for c in range(NH):
            @block.sync
            def _(g, half=half, c=c):
                if half == 1 and c == 0:
                    g.wait_ge(sems["spm"].h, NB)
                g.dma_start_transpose(
                    out=woT[:, c, :],
                    in_=won_d[half * 1024:(half + 1) * 1024, c * 128:(c + 1) * 128]
                ).then_inc(sems["spl"].h, 16)
            sems["spl"].n += 16
        wo_done = sems["spl"].n

        for b in range(NB):
            hb = b % 2
            bank0 = 2 * (b % 2)

            @block.sync
            def _(e, half=half, b=b, hb=hb):
                if half * NB + b >= 2:
                    e.wait_ge(sems["spm"].h, half * NB + b - 1)
                e.dma_start(out=otsl[hb][:], in_=ot_r[:, :, b * 128:(b + 1) * 128]
                            ).then_inc(sems["ath"].h, 16)
            sems["ath"].n += 16
            olv = sems["ath"].n

            @block.tensor
            def _(t, half=half, b=b, hb=hb, bank0=bank0, olv=olv, wo_done=wo_done):
                t.wait_ge(sems["ath"].h, olv)
                t.wait_ge(sems["spl"].h, wo_done)
                if half * NB + b >= 2:
                    t.wait_ge(sems["spv"].h, half * NB + b - 1)
                for j in range(2):
                    for hh in range(NH):
                        lhsT = otsl[hb][:, hh, :]
                        mm = t.matmul(ps[bank0 + j][:], lhsT,
                                      woT[:, hh, j * 512:(j + 1) * 512],
                                      start=(hh == 0), stop=(hh == NH - 1))
                mm.then_inc(sems["spm"].h, 1)
            sems["spm"].n += 1
            smv = sems["spm"].n

            @block.scalar
            def _(s, half=half, b=b, bank0=bank0, smv=smv):
                s.wait_ge(sems["spm"].h, smv)
                if half * NB + b >= 2:
                    s.wait_ge(sems["spw"].h, 16 * (half * NB + b - 1))
                for j in range(2):
                    mm = s.activation(out=sev[b % 2][:, j * 512:(j + 1) * 512],
                                      in_=ps[bank0 + j][:], func=AF.Copy)
                mm.then_inc(sems["spv"].h, 1)
            sems["spv"].n += 1
            spv_ = sems["spv"].n

            @block.sync
            def _(e, half=half, b=b, spv_=spv_):
                e.wait_ge(sems["spv"].h, spv_)
                e.dma_start(out=s_t[b][:, half * 1024:(half + 1) * 1024],
                            in_=sev[b % 2][:]).then_inc(sems["spw"].h, 16)
            sems["spw"].n += 16


    for b in range(NB):
        par = b % 2
        ldv2 = LQ1

        hav0 = sems["lqa"].n
        for tcol in range(4):
            tb = tcol % 2
            idx = b * 4 + tcol

            @block.sync
            def _(e, b=b, tcol=tcol, tb=tb, idx=idx):
                if idx >= 2:
                    e.wait_ge(sems["lqz"].h, idx - 1)
                e.dma_start(out=habuf[tb][:],
                            in_=ha_t[b][:, tcol * 2048:(tcol + 1) * 2048]
                            ).then_inc(sems["lqa"].h, 16)
            sems["lqa"].n += 16

        @block.scalar
        def _(s, b=b, par=par, ldv2=ldv2, hav0=hav0):
            for tcol in range(4):
                tb = tcol % 2
                idx = b * 4 + tcol
                s.wait_ge(sems["lqa"].h, hav0 + 16 * (tcol + 1))
                s.wait_ge(sems["lqd"].h, ldv2)
                if idx >= 1:
                    s.wait_ge(sems["lqg"].h, idx)
                s.activation(out=ztw[tb][:], in_=habuf[tb][:], func=AF.Identity,
                             scale=bst_r[:, b:b + 1],
                             bias=bst_b[:, b:b + 1]).then_inc(sems["lqz"].h, 1)
        sems["lqz"].n += 4

        for tcol in range(4):
            tb = tcol % 2
            lev = sems["lqz"].n - 3 + tcol

            @block.vector
            def _(v, b=b, par=par, tcol=tcol, tb=tb, lev=lev, gb_done=gb_done):
                v.wait_ge(sems["lqz"].h, lev)
                v.wait_ge(sems["lqa"].h, gb_done)
                if b >= 1:
                    v.wait_ge(sems["lqf"].h, 8 * b)
                sl = slice(tcol * 2048, (tcol + 1) * 2048)
                v.tensor_tensor(out=ztw[tb][:], in0=ztw[tb][:], in1=gbc[:, sl],
                                op=ALU.mult)
                v.drain()
                v.tensor_tensor(out=hlf[:, sl], in0=ztw[tb][:], in1=bbc[:, sl],
                                op=ALU.add)
                v.drain()
                v.reduce_max(out=SV(9 + tcol, par), in_=hlf[:, sl], axis=AXX,
                             apply_absolute_value=True).then_inc(sems["lqg"].h, 1)
            sems["lqg"].n += 1
        lgv_all = sems["lqg"].n

        @block.scalar
        def _(s, b=b, par=par, lgv_all=lgv_all, lbv2=sems["lqb"].n + 1):
            s.wait_ge(sems["lqg"].h, lgv_all)
            for tcol in range(4):
                tb = tcol % 2
                sl = slice(tcol * 2048, (tcol + 1) * 2048)
                s.activation(out=ztw[tb][:], in_=hlf[:, sl], func=AF.Square,
                             accum_out=SV(13 + tcol, par)).then_inc(sems["lqf"].h, 1)
            s.wait_ge(sems["lqb"].h, lbv2)
            s.activation(out=SV(14, par), in_=SV(13, par), func=AF.Sqrt).then_inc(sems["lqc"].h, 1)
        sems["lqf"].n += 4
        lfv = sems["lqf"].n

        @block.vector
        def _(v, b=b, par=par, lfv=lfv):
            v.wait_ge(sems["lqf"].h, lfv)
            v.tensor_add(out=SV(13, par), in0=SV(13, par), in1=SV(14, par))
            v.tensor_add(out=SV(15, par), in0=SV(15, par), in1=SV(16, par))
            v.drain()
            v.tensor_add(out=SV(13, par), in0=SV(13, par), in1=SV(15, par))
            v.tensor_max(out=SV(9, par), in0=SV(9, par), in1=SV(10, par))
            v.tensor_max(out=SV(11, par), in0=SV(11, par), in1=SV(12, par))
            v.drain()
            v.tensor_max(out=SV(9, par), in0=SV(9, par), in1=SV(11, par))
            v.drain()
            v.tensor_scalar_mul(out=SV(13, par), in0=SV(13, par), scalar1=1.0 / INNER)
            v.drain()
            v.tensor_scalar_add(out=SV(13, par), in0=SV(13, par), scalar1=RMS_EPS).then_inc(sems["lqb"].h, 1)
        sems["lqb"].n += 1
        lbv2 = sems["lqb"].n

        sems["lqc"].n += 1
        lcv2 = sems["lqc"].n

        @block.vector
        def _(v, b=b, par=par, lcv2=lcv2):
            v.wait_ge(sems["lqc"].h, lcv2)
            v.reciprocal(out=SV(15, par), in_=SV(14, par))
            v.drain()
            v.tensor_mul(out=SV(16, par), in0=SV(15, par), in1=SV(15, par))
            v.drain()
            v.tensor_mul(out=SV(16, par), in0=SV(16, par), in1=SV(13, par))
            v.drain()
            v.tensor_scalar(out=SV(16, par), in0=SV(16, par), scalar1=-0.5, scalar2=1.5,
                            op0=ALU.mult, op1=ALU.add)
            v.drain()
            v.tensor_mul(out=SV(15, par), in0=SV(15, par), in1=SV(16, par))
            v.drain()
            v.tensor_mul(out=SV(17, par), in0=SV(9, par), in1=SV(15, par))
            v.drain()
            v.tensor_scalar_max(out=SV(17, par), in0=SV(17, par), scalar1=ACT_CLIP)
            v.drain()
            v.reciprocal(out=SV(18, par), in_=SV(17, par))
            v.drain()
            v.tensor_scalar_mul(out=SV(18, par), in0=SV(18, par), scalar1=127.0)
            v.drain()
            v.tensor_mul(out=SV(19, par), in0=SV(15, par), in1=SV(18, par))
            v.tensor_scalar_mul(out=rscale2[:, b:b + 1], in0=SV(17, par),
                                scalar1=1.0 / 127.0)
            v.drain()
            v.tensor_mul(out=alpha2[:, b:b + 1], in0=rscale2[:, b:b + 1],
                         in1=mw_col[:, 1:2]).then_inc(sems["lqd"].h, 1)
        sems["lqd"].n += 1
        ldv3 = sems["lqd"].n

        @block.scalar
        def _(s, b=b, par=par, ldv3=ldv3):
            s.wait_ge(sems["lqd"].h, ldv3)
            for tcol in range(4):
                tb = tcol % 2
                idx = b * 4 + tcol
                if idx >= 1:
                    s.wait_ge(sems["lqh"].h, idx)
                sl = slice(tcol * 2048, (tcol + 1) * 2048)
                s.activation(out=hqi[tb][:], in_=hlf[:, sl], func=AF.Copy,
                             scale=SV(19, par)).then_inc(sems["lqf"].h, 1)
        sems["lqf"].n += 4

        for tcol in range(4):
            tb = tcol % 2
            idx = b * 4 + tcol
            lfv2 = sems["lqf"].n - 3 + tcol

            @block.vector
            def _(v, tb=tb, lfv2=lfv2, idx=idx):
                v.wait_ge(sems["lqf"].h, lfv2)
                if idx >= 1:
                    v.wait_ge(sems["lqw"].h, 16 * idx)
                v.tensor_copy(out=hqb[tb][:], in_=hqi[tb][:]).then_inc(sems["lqh"].h, 1)
            sems["lqh"].n += 1
            lev3 = sems["lqh"].n

            @block.sync
            def _(e, b=b, tcol=tcol, tb=tb, lev3=lev3):
                e.wait_ge(sems["lqh"].h, lev3)
                e.dma_start(out=hq_t[b][:, tcol * 2048:(tcol + 1) * 2048],
                            in_=hqb[tb][:]).then_inc(sems["lqw"].h, 16)
            sems["lqw"].n += 16

    @block.sync
    def _(e):
        e.wait_ge(sems["lqw"].h, sems["lqw"].n)
        e.wait_ge(sems["spw"].h, sems["spw"].n)

    barrier()
    if phases <= 5:
        _scope_push(nc, None)
        return

    _scope_cm = _scope_push(nc, "p6")
    region()
    acc = A.alloc((128, 4, 2048), F32, "acc")
    w2s = [A.alloc((128, 8, 2048), BF16, f"w2slab{i}") for i in range(2)]
    hqs = [A.alloc((128, 8, 512), BF16, f"hqslab{i}") for i in range(2)]
    b2bc = A.alloc((128, 2048), F32, "b2bc")
    tf_ = A.alloc((128, 2048), F32, "tf")
    gf_ = A.alloc((128, 2048), F32, "gf")
    xrow = [A.alloc((128, 2048), F32, f"xrow{i}") for i in range(2)]
    srow2 = [A.alloc((128, 2048), F32, f"srow2_{i}") for i in range(2)]
    orow0 = A.alloc((128, 2048), F32, "orow0")
    orow = [orow0, orow0]

    b2_r = b2_d.rearrange("(a i) -> a i", a=1)
    aux1_b6 = sems["aux1"].n
    aux2_b6 = sems["aux2"].n

    @block.sync
    def _(e):
        e.dma_start(out=b2bc[:], in_=b2_r.broadcast_to((128, 2048))
                    ).then_inc(sems["f2h"].h, 16)
    sems["f2h"].n += 16
    b2done = sems["f2h"].n

    out_t = out_d.rearrange("(b p) d -> b p d", p=128)

    NG2 = 8
    for nq in range(4):
        for icg in range(NG2):
            gno = nq * NG2 + icg
            wb = w2s[gno % 2]
            hb2 = hqs[gno % 2]
            for i in range(8):
                ic = icg * 8 + i

                @block.sync
                def _(e, nq=nq, i=i, ic=ic, gno=gno, wb=wb, hb2=hb2):
                    if gno >= 2 and i == 0:
                        e.wait_ge(sems["f2m"].h, 4 * (gno - 1))
                    e.dma_start_transpose(out=wb[:, i, :],
                                          in_=w2q_d[:, ic * 128:(ic + 1) * 128]
                                          ).then_inc(sems["f2w"].h, 16)
                    e.dma_start_transpose(out=hb2[:, i, :],
                                          in_=hq_d[nq * 512:(nq + 1) * 512,
                                                   ic * 128:(ic + 1) * 128]
                                          ).then_inc(sems["f2h"].h, 16)
                sems["f2w"].n += 16
                sems["f2h"].n += 16
            ld_w = sems["f2w"].n
            ld_h = sems["f2h"].n

            for bl in range(4):
                bank0 = 4 * (bl % 2)

                @block.tensor
                def _(t, bl=bl, bank0=bank0, ld_w=ld_w, ld_h=ld_h, gno=gno,
                      wb=wb, hb2=hb2):
                    t.wait_ge(sems["f2w"].h, ld_w)
                    t.wait_ge(sems["f2h"].h, ld_h)
                    gidx = gno * 4 + bl
                    if gidx >= 2:
                        t.wait_ge(sems["f2v"].h, 4 * (gidx - 1))
                    for j in range(4):
                        for i in range(8):
                            lhsT = hb2[:, i, bl * 128:(bl + 1) * 128]
                            mm = t.matmul(ps[bank0 + j][:], lhsT,
                                          wb[:, i, j * 512:(j + 1) * 512],
                                          start=(i == 0), stop=(i == 7))
                    mm.then_inc(sems["f2m"].h, 1)
                sems["f2m"].n += 1
                fmv = sems["f2m"].n

                @block.vector
                def _(v, icg=icg, bl=bl, bank0=bank0, fmv=fmv):
                    v.wait_ge(sems["f2m"].h, fmv)
                    for j in range(4):
                        dst = acc[:, bl, j * 512:(j + 1) * 512]
                        if icg == 0:
                            v.tensor_copy(out=dst, in_=ps[bank0 + j][:])
                        else:
                            v.tensor_add(out=dst, in0=dst, in1=ps[bank0 + j][:])
                    v.drain()
                    v.sem_inc(sems["f2v"].h, 4)
                sems["f2v"].n += 4

        for bl in range(4):
            nb_g = nq * 4 + bl
            eb = bl % 2

            @block.sync
            def _(e, nq=nq, bl=bl, eb=eb, nb_g=nb_g):
                if nb_g >= 2:
                    e.wait_ge(sems["f2e"].h, nb_g - 1)
                e.dma_start(out=xrow[eb][:], in_=x_t[nb_g]).then_inc(sems["f2x"].h, 16)
                e.dma_start(out=srow2[eb][:], in_=s_t[nb_g]).then_inc(sems["f2x"].h, 16)
            sems["f2x"].n += 32
            xlv = sems["f2x"].n

            @block.vector
            def _(v, nq=nq, bl=bl, nb_g=nb_g, b2done=b2done, aux2_b6=aux2_b6):
                v.wait_ge(sems["f2h"].h, b2done)
                if nb_g >= 1:
                    v.wait_ge(sems["aux2"].h, aux2_b6 + nb_g)
                v.scalar_tensor_tensor(out=tf_[:], in0=acc[:, bl, :],
                                       scalar=alpha2[:, nb_g:nb_g + 1], in1=b2bc[:],
                                       op0=ALU.mult, op1=ALU.add).then_inc(sems["aux1"].h, 1)
            sems["aux1"].n += 1
            gav = sems["aux1"].n

            @block.scalar
            def _(s, gav=gav, nb_g=nb_g):
                s.wait_ge(sems["aux1"].h, gav)
                if nb_g >= 1:
                    s.wait_ge(sems["f2e"].h, nb_g)
                s.activation(out=gf_[:], in_=tf_[:], func=AF.Silu).then_inc(sems["aux2"].h, 1)
            sems["aux2"].n += 1
            gsv = sems["aux2"].n

            @block.vector
            def _(v, eb=eb, gsv=gsv, xlv=xlv, nb_g=nb_g):
                v.wait_ge(sems["aux2"].h, gsv)
                v.wait_ge(sems["f2x"].h, xlv)
                if nb_g >= 1:
                    v.wait_ge(sems["outw"].h, 16 * nb_g)
                v.tensor_tensor(out=orow[eb][:], in0=gf_[:], in1=srow2[eb][:],
                                op=ALU.mult)
                v.drain()
                v.tensor_add(out=orow[eb][:], in0=orow[eb][:], in1=xrow[eb][:]).then_inc(sems["f2e"].h, 1)
            sems["f2e"].n += 1
            fev = sems["f2e"].n

            @block.sync
            def _(e, eb=eb, nb_g=nb_g, fev=fev):
                e.wait_ge(sems["f2e"].h, fev)
                e.dma_start(out=out_t[nb_g], in_=orow[eb][:]).then_inc(sems["outw"].h, 16)
            sems["outw"].n += 16

    @block.sync
    def _(e):
        e.wait_ge(sems["outw"].h, sems["outw"].n)
    _scope_push(nc, None)


def kernel(L0_attn_x, L0_attn_Wq, L0_attn_Wk, L0_attn_Wv, L0_attn_Wo,
           L0_ffn_w1, L0_ffn_b1, L0_ffn_ln_g, L0_ffn_ln_b, L0_ffn_w2, L0_ffn_b2):
    nc = build_program()
    xs = np.ascontiguousarray(np.asarray(L0_attn_x, dtype=np.float32))
    base = {
        "wq": np.ascontiguousarray(L0_attn_Wq, dtype=np.float32),
        "wk": np.ascontiguousarray(L0_attn_Wk, dtype=np.float32),
        "wv": np.ascontiguousarray(L0_attn_Wv, dtype=np.float32),
        "wo": np.ascontiguousarray(L0_attn_Wo, dtype=np.float32),
        "w1": np.ascontiguousarray(L0_ffn_w1, dtype=np.float32),
        "b1": np.ascontiguousarray(L0_ffn_b1, dtype=np.float32),
        "lng": np.ascontiguousarray(L0_ffn_ln_g, dtype=np.float32),
        "lnb": np.ascontiguousarray(L0_ffn_ln_b, dtype=np.float32),
        "w2": np.ascontiguousarray(L0_ffn_w2, dtype=np.float32),
        "b2": np.ascontiguousarray(L0_ffn_b2, dtype=np.float32),
    }
    in_maps = [dict(base, x=xs[i]) for i in range(8)]
    res = run_bass_kernel_spmd(nc, in_maps, core_ids=list(range(8)))
    return np.stack([np.asarray(r["out"]) for r in res.results], axis=0)
